# version 03
# baseline (speedup 1.0000x reference)
import contextlib
import numpy as np
import concourse.bass as bass
import concourse.mybir as mybir
from concourse.bass_utils import run_bass_kernel_spmd

F32 = mybir.dt.float32
BF16 = mybir.dt.bfloat16
U32 = mybir.dt.uint32
I16 = mybir.dt.int16
I32 = mybir.dt.int32
AF = mybir.ActivationFunctionType
ALU = mybir.AluOpType
AX = mybir.AxisListType

ENGINES = ("tensor", "vector", "scalar", "gpsimd", "sync")
DMA_QUEUES = {"sync": 6, "gpsimd": 6, "scalar": 4}


class Step:
    __slots__ = ("idx", "eng", "fn", "reads", "writes", "is_dma", "deps",
                 "need_inc", "inc_no", "dma_slot", "dma_use_no", "tag")

    def __init__(self, idx, eng, fn, reads, writes, is_dma, tag):
        self.idx = idx
        self.eng = eng
        self.fn = fn
        self.reads = reads
        self.writes = writes
        self.is_dma = is_dma
        self.deps = set()
        self.need_inc = False
        self.inc_no = None
        self.dma_slot = None
        self.dma_use_no = None
        self.tag = tag


class Prog:

    def __init__(self, name="prog"):
        self.nc = bass.Bass("TRN2", target_bir_lowering=False)
        self.stack = contextlib.ExitStack()
        self.steps = []
        self._writer = {}
        self._readers = {}
        self._dma_count = {q: 0 for q in DMA_QUEUES}

    def dram(self, name, shape, dtype, kind="ExternalInput"):
        return self.nc.dram_tensor(name, list(shape), dtype, kind=kind).ap()

    _uid = 0

    def _uname(self, name):
        Prog._uid += 1
        return f"{name}_u{Prog._uid}"

    def sbuf(self, name, shape, dtype):
        return self.stack.enter_context(
            self.nc.sbuf_tensor(self._uname(name), list(shape), dtype))

    def psum(self, name, shape, dtype=F32):
        esz = 4 if dtype == F32 else 2
        per = shape[-1] * esz
        pad = (-per) % 2048
        pshape = list(shape)
        pshape[-1] += pad // esz
        h = self.stack.enter_context(
            self.nc.psum_tensor(self._uname(name), pshape, dtype))
        if pad == 0:
            return h
        idx = tuple(slice(0, s) for s in shape)
        return h[idx]

    def sem(self, name):
        return self.stack.enter_context(self.nc.semaphore(name))

    def step(self, eng, fn, r=(), w=(), dma=False, tag=""):
        assert eng in ENGINES
        if isinstance(r, str):
            r = (r,)
        if isinstance(w, str):
            w = (w,)
        st = Step(len(self.steps), eng, fn, tuple(r), tuple(w), dma, tag)
        for k in st.reads:
            lw = self._writer.get(k)
            if lw is not None:
                st.deps.add(lw)
        for k in st.writes:
            lw = self._writer.get(k)
            if lw is not None:
                st.deps.add(lw)
            for rd in self._readers.get(k, ()):
                st.deps.add(rd)
        st.deps.discard(st.idx)
        for k in st.reads:
            self._readers.setdefault(k, []).append(st.idx)
        for k in st.writes:
            self._writer[k] = st.idx
            self._readers[k] = []
        if dma:
            q = eng
            st.dma_slot = self._dma_count[q] % DMA_QUEUES[q]
            st.dma_use_no = self._dma_count[q] // DMA_QUEUES[q]
            self._dma_count[q] += 1
        self.steps.append(st)
        return st.idx

    def dma(self, queue, out, in_, r=(), w=(), tag=""):
        def fn(E, out=out, in_=in_):
            return E.dma_start(out=out, in_=in_)
        return self.step(queue, fn, r=r, w=w, dma=True, tag=tag)

    def barrier(self, tag="bar"):
        last = {}
        last_dma = {}
        for st in self.steps:
            last[st.eng] = st.idx
            if st.is_dma:
                last_dma[(st.eng, st.dma_slot)] = st.idx
        deps = set(last.values()) | set(last_dma.values())
        for eng in ENGINES:
            def fn(E):
                return E.nop()
            i = self.step(eng, fn, tag=tag + "_" + eng)
            self.steps[i].deps |= (deps - {i})

    @contextlib.contextmanager
    def scope(self):
        inner = contextlib.ExitStack()
        outer = self.stack
        self.stack = inner
        try:
            yield
        finally:
            self.stack = outer
            self.barrier()
            inner.close()

    def emit(self, strict_same_engine=None):
        import os
        if strict_same_engine is None:
            strict_same_engine = os.environ.get('STRICT_EDGES', '1') != '0'
        self._noprune = os.environ.get('NO_PRUNE', '1') != '0'
        nc = self.nc
        for st in self.steps:
            pruned = set()
            for d in st.deps:
                p = self.steps[d]
                if (not strict_same_engine) and (not p.is_dma) and (not st.is_dma) \
                        and p.eng == st.eng:
                    continue
                if (not p.is_dma) and st.is_dma and p.eng == st.eng:
                    pruned.add(d)
                    continue
                pruned.add(d)
            st.deps = pruned
        for st in self.steps:
            for d in st.deps:
                p = self.steps[d]
                if not p.is_dma:
                    p.need_inc = True
        ctr = {e: 0 for e in ENGINES}
        for st in self.steps:
            if (not st.is_dma) and st.need_inc:
                ctr[st.eng] += 1
                st.inc_no = ctr[st.eng]
        eng_sem = {e: self.sem(f"sem_{e}") for e in ENGINES if ctr[e] > 0}
        dma_sems = {}
        for q, n in DMA_QUEUES.items():
            if self._dma_count[q]:
                dma_sems[q] = [self.sem(f"dsem_{q}{i}") for i in range(n)]
        streams = {e: [] for e in ENGINES}
        for st in self.steps:
            waits = {}
            for d in sorted(st.deps):
                p = self.steps[d]
                if p.is_dma:
                    key = ("d", p.eng, p.dma_slot)
                    val = 16 * (p.dma_use_no + 1)
                else:
                    key = ("c", p.eng)
                    val = p.inc_no
                if waits.get(key, 0) < val:
                    waits[key] = val
            streams[st.eng].append((st, waits))

        blockmap = {"tensor": "tensor", "vector": "vector", "scalar": "scalar",
                    "gpsimd": "gpsimd", "sync": "sync"}
        hiwater = {e: {} for e in ENGINES}

        def make_body(eng):
            def body(E):
                hw = hiwater[eng]
                for st, waits in streams[eng]:
                    for key, val in sorted(waits.items(), key=lambda kv: str(kv[0])):
                        if (not self._noprune) and hw.get(key, 0) >= val:
                            continue
                        hw[key] = val
                        if key[0] == "d":
                            E.wait_ge(dma_sems[key[1]][key[2]], val)
                        else:
                            E.wait_ge(eng_sem[key[1]], val)
                    if st.is_dma:
                        skey = ("d", eng, st.dma_slot)
                        sval = 16 * st.dma_use_no
                        if sval > 0 and hw.get(skey, 0) < sval:
                            hw[skey] = sval
                            E.wait_ge(dma_sems[eng][st.dma_slot], sval)
                    inst = st.fn(E)
                    if st.is_dma:
                        inst.then_inc(dma_sems[eng][st.dma_slot], 16)
                    elif st.need_inc:
                        inst.then_inc(eng_sem[eng], 1)
            return body

        with nc.Block() as block:
            for eng in ENGINES:
                if not streams[eng]:
                    continue
                getattr(block, blockmap[eng])(make_body(eng))
        return nc

    def finish(self):
        self.stack.close()


D = 1024
DC = 8
T = 4096
S = 2048
NB = 2
BIG = 1e30


class Ctx:

    def __init__(self, P: Prog):
        self.P = P
        nc = P.nc
        self.ones128_f = P.sbuf("c_ones128f", [128, 1], F32)
        self.onesrow_b = P.sbuf("c_onesrowb", [1, 128], BF16)
        self.onesrow_f = P.sbuf("c_onesrowf", [1, 128], F32)
        P.step("vector", lambda E: E.memset(self.ones128_f[:], 1.0), w="c0")
        P.step("vector", lambda E: E.memset(self.onesrow_b[:], 1.0), w="c1")
        P.step("vector", lambda E: E.memset(self.onesrow_f[:], 1.0), w="c2")
        self.const_keys = ("c0", "c1", "c2")

    def load_w_bf16(self, dram_ap, sb, key, rows=None, tag="w"):
        P = self.P
        P.dma("gpsimd", sb, dram_ap, r=(), w=key, tag=tag)

    def mm(self, psum_ap, lhsT_ap, rhs_ap, start, stop, r=(), w=(), tag="mm"):
        P = self.P
        def fn(E, o=psum_ap, l=lhsT_ap, rh=rhs_ap, st=start, sp=stop):
            return E.matmul(o, l, rh, start=st, stop=sp)
        return P.step("tensor", fn, r=r, w=w, tag=tag)


def silu_combine_store(ctx, x_dram, s_dram, f_dram, out_dram, extra_add_dram=None):
    P = ctx.P
    xt = P.sbuf("cmb_x", [128, 512], F32)
    st = P.sbuf("cmb_s", [128, 512], F32)
    ft = P.sbuf("cmb_f", [128, 512], F32)
    et = P.sbuf("cmb_e", [128, 512], F32)
    gt = P.sbuf("cmb_g", [128, 512], F32)
    ot = P.sbuf("cmb_o", [128, 512], F32)
    for c in range(DC):
        for tb in range(T // 512):
            sl = (c, slice(None), slice(tb * 512, (tb + 1) * 512))
            kx, ks, kf, ke = "cmb_x", "cmb_s", "cmb_f", "cmb_e"
            P.dma("sync", xt[:], x_dram[sl], w=kx)
            P.dma("sync", st[:], s_dram[sl], w=ks)
            P.dma("sync", ft[:], f_dram[sl], w=kf)
            if extra_add_dram is not None:
                P.dma("sync", et[:], extra_add_dram[sl], w=ke)
                P.step("vector", lambda E, a=ft, b=et: E.tensor_add(a[:], a[:], b[:]),
                       r=(kf, ke), w=kf)
            P.step("scalar", lambda E, a=gt, b=ft: E.activation(out=a[:], in_=b[:], func=AF.Silu),
                   r=kf, w="cmb_gk")
            P.step("vector", lambda E, a=gt, b=st: E.tensor_mul(a[:], a[:], b[:]),
                   r=("cmb_gk", ks), w="cmb_gk2")
            P.step("vector", lambda E, a=ot, b=xt, g=gt: E.tensor_add(a[:], b[:], g[:]),
                   r=("cmb_gk2", kx), w="cmb_ok")
            P.dma("sync", out_dram[sl], ot[:], r="cmb_ok", w="cmb_out")


def ln_phase(P, ctx, x_dram, xn_dram, h_dram=None, lnw_cols=None, pfx="ln", xn32_dram=None,
             gamma_cols=None, beta_cols=None, eps=1e-5):
    with P.scope():
        xt = [P.sbuf(f"{pfx}_x{c}", [128, 512], F32) for c in range(DC)]
        sq = P.sbuf(f"{pfx}_sq", [128, 512], F32)
        srow = P.sbuf(f"{pfx}_srow", [1, 512], F32)
        qrow = P.sbuf(f"{pfx}_qrow", [1, 512], F32)
        mrow = P.sbuf(f"{pfx}_mrow", [1, 512], F32)
        var = P.sbuf(f"{pfx}_var", [1, 512], F32)
        tmp = P.sbuf(f"{pfx}_tmp", [1, 512], F32)
        rrow = P.sbuf(f"{pfx}_rrow", [1, 512], F32)
        cbrow = P.sbuf(f"{pfx}_cbrow", [1, 512], F32)
        t1 = P.sbuf(f"{pfx}_t1", [128, 512], F32)
        ob = P.sbuf(f"{pfx}_ob", [128, 512], BF16)
        ps_s = P.psum(f"{pfx}_ps_s", [1, 512])
        ps_q = P.psum(f"{pfx}_ps_q", [1, 512])
        ps_m = P.psum(f"{pfx}_ps_m", [128, 512])
        ps_r = P.psum(f"{pfx}_ps_r", [128, 512])
        ps_cb = P.psum(f"{pfx}_ps_cb", [128, 512])
        for tb in range(T // 512):
            tsl = slice(tb * 512, (tb + 1) * 512)
            for c in range(DC):
                P.dma("sync", xt[c][:], x_dram[c, :, tsl], w=f"{pfx}x{c}")
            for c in range(DC):
                P.step("tensor", lambda E, c=c: E.matmul(ps_s[:], ctx.ones128_f[:], xt[c][:],
                       start=(c == 0), stop=(c == DC - 1)), r=(f"{pfx}x{c}", "c0"), w=f"{pfx}pss")
            for c in range(DC):
                P.step("scalar", lambda E, c=c: E.activation(out=sq[:], in_=xt[c][:], func=AF.Square),
                       r=f"{pfx}x{c}", w=f"{pfx}sq")
                P.step("tensor", lambda E, c=c: E.matmul(ps_q[:], ctx.ones128_f[:], sq[:],
                       start=(c == 0), stop=(c == DC - 1)), r=(f"{pfx}sq", "c0"), w=f"{pfx}psq")
            P.step("vector", lambda E: E.tensor_scalar(out=mrow[:], in0=ps_s[:], scalar1=1.0 / D,
                   scalar2=None, op0=ALU.mult), r=f"{pfx}pss", w=f"{pfx}m")
            P.step("vector", lambda E: E.tensor_scalar(out=qrow[:], in0=ps_q[:], scalar1=1.0 / D,
                   scalar2=None, op0=ALU.mult), r=f"{pfx}psq", w=f"{pfx}q")
            P.step("vector", lambda E: E.tensor_tensor(out=var[:], in0=mrow[:], in1=mrow[:], op=ALU.mult),
                   r=f"{pfx}m", w=f"{pfx}v0")
            P.step("vector", lambda E: E.tensor_sub(var[:], qrow[:], var[:]),
                   r=(f"{pfx}q", f"{pfx}v0"), w=f"{pfx}v")
            P.step("vector", lambda E: E.tensor_scalar(out=tmp[:], in0=var[:], scalar1=eps,
                   scalar2=None, op0=ALU.add), r=f"{pfx}v", w=f"{pfx}t")
            P.step("scalar", lambda E: E.activation(out=srow[:], in_=tmp[:], func=AF.Sqrt),
                   r=f"{pfx}t", w=f"{pfx}sd")
            P.step("vector", lambda E: E.reciprocal(rrow[:], srow[:]), r=f"{pfx}sd", w=f"{pfx}r")
            if h_dram is not None:
                P.step("vector", lambda E: E.tensor_tensor(out=tmp[:], in0=rrow[:], in1=rrow[:], op=ALU.mult),
                       r=f"{pfx}r", w=f"{pfx}r2")
                P.step("vector", lambda E: E.tensor_tensor(out=tmp[:], in0=tmp[:], in1=var[:], op=ALU.mult),
                       r=(f"{pfx}r2", f"{pfx}v"), w=f"{pfx}ms")
                P.step("vector", lambda E: E.tensor_scalar(out=tmp[:], in0=tmp[:], scalar1=1e-6,
                       scalar2=None, op0=ALU.add), r=f"{pfx}ms", w=f"{pfx}ms2")
                P.step("scalar", lambda E: E.activation(out=tmp[:], in_=tmp[:], func=AF.Sqrt),
                       r=f"{pfx}ms2", w=f"{pfx}ms3")
                P.step("vector", lambda E: E.reciprocal(cbrow[:], tmp[:]), r=f"{pfx}ms3", w=f"{pfx}rr")
                if xn32_dram is None:
                    P.step("vector", lambda E: E.tensor_tensor(out=cbrow[:], in0=cbrow[:],
                           in1=rrow[:], op=ALU.mult), r=(f"{pfx}rr", f"{pfx}r"), w=f"{pfx}cb")
                else:
                    P.step("vector", lambda E: E.engine_nop(), r=f"{pfx}rr", w=f"{pfx}cb")
            P.step("tensor", lambda E: E.matmul(ps_m[:], ctx.onesrow_f[:], mrow[:], start=True, stop=True),
                   r=(f"{pfx}m", "c2"), w=f"{pfx}bm")
            P.step("tensor", lambda E: E.matmul(ps_r[:], ctx.onesrow_f[:], rrow[:], start=True, stop=True),
                   r=(f"{pfx}r", "c2"), w=f"{pfx}br")
            if h_dram is not None:
                P.step("tensor", lambda E: E.matmul(ps_cb[:], ctx.onesrow_f[:], cbrow[:], start=True, stop=True),
                       r=(f"{pfx}cb", "c2"), w=f"{pfx}bcb")
            for c in range(DC):
                P.step("vector", lambda E, c=c: E.tensor_sub(t1[:], xt[c][:], ps_m[:]),
                       r=(f"{pfx}x{c}", f"{pfx}bm"), w=f"{pfx}t1")
                if xn32_dram is not None:
                    P.step("vector", lambda E: E.tensor_tensor(out=t1[:], in0=t1[:], in1=ps_r[:],
                           op=ALU.mult), r=(f"{pfx}t1", f"{pfx}br"), w=f"{pfx}t1")
                    P.dma("sync", xn32_dram[c, :, tsl], t1[:], r=f"{pfx}t1", w=f"{pfx}o32{c}")
                    P.step("vector", lambda E: E.tensor_copy(ob[:], t1[:]),
                           r=f"{pfx}t1", w=f"{pfx}ob")
                else:
                    P.step("vector", lambda E: E.tensor_tensor(out=ob[:], in0=t1[:], in1=ps_r[:],
                           op=ALU.mult), r=(f"{pfx}t1", f"{pfx}br"), w=f"{pfx}ob")
                if gamma_cols is not None:
                    P.step("vector", lambda E, c=c: E.tensor_scalar(out=ob[:], in0=ob[:],
                           scalar1=gamma_cols[:, c:c + 1], scalar2=beta_cols[:, c:c + 1],
                           op0=ALU.mult, op1=ALU.add), r=(f"{pfx}ob", "lngb"), w=f"{pfx}ob")
                P.dma("sync", xn_dram[c, :, tsl], ob[:], r=f"{pfx}ob", w=f"{pfx}od{c}")
                if h_dram is not None:
                    P.step("vector", lambda E, c=c: E.scalar_tensor_tensor(
                        out=ob[:], in0=t1[:], scalar=lnw_cols[:, c:c + 1], in1=ps_cb[:],
                        op0=ALU.mult, op1=ALU.mult),
                        r=(f"{pfx}t1", f"{pfx}bcb", "lnw"), w=f"{pfx}ob")
                    P.dma("sync", h_dram[c, :, tsl], ob[:], r=f"{pfx}ob", w=f"{pfx}hd{c}")


def flash_attn_T(P, ctx, pfx, b, heads, kf_head, qf_head, vtm_slice, negm_col,
                 out_write, n_ktiles, causal, maskc=None, gate=None, ksplit=None):
    W = P.sbuf(f"{pfx}_W", [128, 512], BF16)
    ps_s = [P.psum(f"{pfx}_ps_s{i}", [128, 512]) for i in range(2)]
    ps_o = P.psum(f"{pfx}_ps_o", [65, 512])
    ps_rb = P.psum(f"{pfx}_ps_rb", [64, 512])
    o_sb = P.sbuf(f"{pfx}_osb", [65, 512], F32)
    rrow = P.sbuf(f"{pfx}_rrow", [1, 512], F32)
    onorm = P.sbuf(f"{pfx}_onorm", [64, 512], BF16)
    nq = S // 512
    for h in heads:
        kparts = kf_head(h)
        for qb in range(nq):
            kmax = (4 * qb + 4) if causal else n_ktiles
            first_v = True
            for kt in range(kmax):
                pb = ps_s[kt % 2]
                kw = f"{pfx}s{kt % 2}"
                qrhs = qf_head(h, qb)
                for pi, (lf, tp) in enumerate(kparts):
                    P.step("tensor", lambda E, lf=lf, kt=kt, rhs=qrhs[pi], pb=pb, tp=tp,
                           st=(pi == 0), sp=(pi == len(kparts) - 1):
                           E.matmul(pb[:], lf(kt), rhs, start=st, stop=sp, tile_position=tp),
                           r=(f"{pfx}qk", kw), w=kw + "mm", tag="scoremm")
                P.step("scalar", lambda E, pb=pb, h=h: E.activation(
                    out=W[:], in_=pb[:], func=AF.Exp, bias=negm_col(h), scale=0.125),
                    r=(kw + "mm", f"{pfx}negm"), w=f"{pfx}W", tag="exp")
                if causal and kt >= 4 * qb:
                    j = kt - 4 * qb
                    off = (3 - j) * 128
                    P.step("vector", lambda E, off=off: E.tensor_tensor(
                        out=W[:], in0=W[:], in1=maskc[:, off:off + 512], op=ALU.mult),
                        r=(f"{pfx}W", "maskc"), w=f"{pfx}W", tag="cmask")
                P.step("tensor", lambda E, h=h, kt=kt, fv=first_v, kmax=kmax:
                       E.matmul(ps_o[:], vtm_slice(h, kt), W[:], start=fv, stop=(kt == kmax - 1)),
                       r=(f"{pfx}W", f"{pfx}vtm", kw), w=(f"{pfx}pso", kw), tag="vmm")
                first_v = False
            P.step("scalar", lambda E: E.activation(out=o_sb[:], in_=ps_o[:], func=AF.Identity),
                   r=f"{pfx}pso", w=(f"{pfx}osb", f"{pfx}pso"))
            P.step("vector", lambda E: E.reciprocal(rrow[:], o_sb[64:65, :]),
                   r=f"{pfx}osb", w=f"{pfx}rr")
            P.step("tensor", lambda E: E.matmul(ps_rb[:], ctx.onesrow_f[0:1, 0:64], rrow[:],
                   start=True, stop=True), r=(f"{pfx}rr", "c2"), w=f"{pfx}rb")
            P.step("vector", lambda E: E.tensor_tensor(out=onorm[:], in0=o_sb[0:64, :],
                   in1=ps_rb[:], op=ALU.mult), r=(f"{pfx}osb", f"{pfx}rb"), w=f"{pfx}on")
            out_write(h, qb, onorm, f"{pfx}on")


def proj_fm(P, ctx, pfx, w_sb, wkey, xin_dram, nin_c, out_cols, bias_cols, bkey,
            out_sb, okey, tpb=512, dtype_out=BF16, extra_evac=None):
    noc = out_cols // 128
    xt = [P.sbuf(f"{pfx}_xt{c}", [128, tpb], BF16) for c in range(nin_c)]
    ps = [P.psum(f"{pfx}_ps{i}", [128, 512]) for i in range(2)]
    for tb in range(T // tpb):
        tsl = slice(tb * tpb, (tb + 1) * tpb)
        for c in range(nin_c):
            P.dma("sync", xt[c][:], xin_dram[c, :, tsl], w=f"{pfx}x{c}")
        for oc in range(noc):
            pb = ps[oc % 2]
            pk = f"{pfx}p{oc % 2}"
            for c in range(nin_c):
                P.step("tensor", lambda E, c=c, oc=oc, pb=pb: E.matmul(
                    pb[:], w_sb[:, c, oc * 128:(oc + 1) * 128], xt[c][:],
                    start=(c == 0), stop=(c == nin_c - 1)),
                    r=(wkey, f"{pfx}x{c}", pk), w=pk, tag="projmm")
            if bias_cols is not None:
                P.step("scalar", lambda E, oc=oc, pb=pb, tsl=tsl: E.activation(
                    out=out_sb[:, oc, tsl], in_=pb[:], func=AF.Identity,
                    bias=bias_cols[:, oc:oc + 1]), r=(pk, bkey), w=(okey, pk))
            else:
                P.step("scalar", lambda E, oc=oc, pb=pb, tsl=tsl: E.activation(
                    out=out_sb[:, oc, tsl], in_=pb[:], func=AF.Identity),
                    r=pk, w=(okey, pk))


ATTN_SUB = "all"
FLASH_NH = 16
FFN_SUB = "all"

def build_layer0(P, ctx, io, phases=("ln", "attn", "ffnattn", "moe", "combine")):
    T2 = S

    if "ln" in phases:
        lnw = P.sbuf("l0_lnw", [128, 8], F32)
        P.dma("sync", lnw[:], io["ln1w"][:, :], w="lnw")
        ln_phase(P, ctx, io["x"], io["xn"], h_dram=io["h"], lnw_cols=lnw, pfx="l0ln",
                 xn32_dram=io["xn32"])

    def _l0_attn(b):
        with P.scope():
            qf = P.sbuf("a_qf", [128, DC, T2], BF16)
            kf = P.sbuf("a_kf", [128, DC, T2], BF16)
            vtm = P.sbuf("a_vtm", [128, 16 * 16 * 65], BF16)
            af = P.sbuf("a_af", [128, DC, T2], BF16)
            negm = P.sbuf("a_negm", [128, 16], F32)
            with P.scope():
                wq = P.sbuf("a_wqkv", [128, DC, 3072], BF16)
                for c in range(DC):
                    P.dma("gpsimd", wq[:, c, :], io["aw_qkv"][c * 128:(c + 1) * 128, :], w="awq")
                bqk = P.sbuf("a_bqk", [128, 16], F32)
                P.dma("sync", bqk[:], io["ab_qk"][:, :], w="abqk")
                bvrep = P.sbuf("a_bvrep", [128, 1040], F32)
                P.dma("sync", bvrep[:], io["ab_vrep"][:, :], w="abv")
                xnb = [P.sbuf(f"a_xnb{c}", [128, 512], BF16) for c in range(DC)]
                ps = [P.psum(f"a_ps{i}", [128, 512]) for i in range(2)]
                for tb in range(T2 // 512):
                    tsl = slice(tb * 512, (tb + 1) * 512)
                    gsl = slice(b * T2 + tb * 512, b * T2 + (tb + 1) * 512)
                    for c in range(DC):
                        P.dma("sync", xnb[c][:], io["xn"][c, :, gsl], w=f"axn{c}")
                    for oc in range(16):
                        pb = ps[oc % 2]
                        pk = f"ap{oc % 2}"
                        for c in range(DC):
                            P.step("tensor", lambda E, c=c, oc=oc, pb=pb, tb=tb: E.matmul(
                                pb[:], wq[:, c, oc * 128:(oc + 1) * 128], xnb[c][:],
                                start=(c == 0), stop=(c == DC - 1)),
                                r=("awq", f"axn{c}", pk), w=pk)
                        dst = qf if oc < 8 else kf
                        dc_ = oc % 8
                        P.step("scalar", lambda E, dst=dst, dc_=dc_, oc=oc, pb=pb, tsl=tsl:
                               E.activation(out=dst[:, dc_, tsl], in_=pb[:], func=AF.Identity,
                               bias=bqk[:, oc:oc + 1]), r=(pk, "abqk"), w=("aqk", pk))
                    for st_ in range(4):
                        tile = tb * 4 + st_
                        for half in range(2):
                            pb = ps[half]
                            pk = f"ap{half}"
                            for c in range(DC):
                                P.step("tensor", lambda E, c=c, st_=st_, half=half, pb=pb: E.matmul(
                                    pb[:], xnb[c][:, st_ * 128:(st_ + 1) * 128],
                                    wq[:, c, 2048 + half * 512: 2048 + (half + 1) * 512],
                                    start=(c == 0), stop=(c == DC - 1)),
                                    r=("awq", f"axn{c}", pk), w=pk)
                            P.step("vector", lambda E, tile=tile, half=half, pb=pb: E.tensor_tensor(
                                out=vtm[:, tile * 1040 + half * 520: tile * 1040 + (half + 1) * 520]
                                    .rearrange("p (h x) -> p h x", x=65)[:, :, 0:64],
                                in0=pb[:].rearrange("p (h x) -> p h x", x=64),
                                in1=bvrep[:, half * 520:(half + 1) * 520]
                                    .rearrange("p (h x) -> p h x", x=65)[:, :, 0:64],
                                op=ALU.add), r=(pk, "abv"), w=("avtm", pk))
                P.step("vector", lambda E: E.memset(
                    vtm[:].rearrange("p (t x) -> p t x", x=65)[:, :, 64:65], 1.0), w="avtm")
                sq = P.sbuf("a_sq", [128, 2048], BF16)
                ps_n2 = P.psum("a_ps_n2", [16, 2048])
                selq = P.sbuf("a_selq", [128, DC * 16], BF16)
                P.dma("sync", selq[:], io["selq"][:, :], w="selq")
                ehrep = P.sbuf("a_ehrep", [16, 16 * 128], F32)
                P.dma("sync", ehrep[:], io["ehrep"][:, :], w="ehrep")
                n2r = P.sbuf("a_n2r", [16, 2], F32)
                mcol = P.sbuf("a_mcol", [16, 1], F32)
                ps_c1 = P.psum("a_ps_c1", [128, 1])
                for ci, src in ((0, qf), (1, kf)):
                    for c in range(DC):
                        P.step("scalar", lambda E, src=src, c=c: E.activation(
                            out=sq[:], in_=src[:, c, :], func=AF.Square), r=("aqk", "asqr"), w="asq")
                        for nb_ in range(4):
                            P.step("tensor", lambda E, c=c, nb_=nb_: E.matmul(
                                ps_n2[:, nb_ * 512:(nb_ + 1) * 512],
                                selq[:, c * 16:(c + 1) * 16], sq[:, nb_ * 512:(nb_ + 1) * 512],
                                start=(c == 0), stop=(c == DC - 1)),
                                r=("asq", "selq", "an2"), w=("an2", "asqr"))
                    P.step("vector", lambda E, ci=ci: E.tensor_reduce(
                        out=n2r[:, ci:ci + 1], in_=ps_n2[:], axis=AX.X, op=ALU.max),
                        r="an2", w=("an2r", "an2"))
                P.step("scalar", lambda E: E.activation(out=n2r[:], in_=n2r[:], func=AF.Sqrt),
                       r="an2r", w="an2r")
                P.step("vector", lambda E: E.tensor_tensor(out=mcol[:], in0=n2r[:, 0:1],
                       in1=n2r[:, 1:2], op=ALU.mult), r="an2r", w="amc0")
                P.step("vector", lambda E: E.tensor_scalar(out=mcol[:], in0=mcol[:],
                       scalar1=-0.125, scalar2=None, op0=ALU.mult), r="amc0", w="amc")
                for h in range(16):
                    P.step("tensor", lambda E, h=h: E.matmul(ps_c1[:], ehrep[:, h * 128:(h + 1) * 128],
                           mcol[:], start=True, stop=True), r=("amc", "ehrep", "anm"), w="anm")
                    P.step("vector", lambda E, h=h: E.tensor_copy(negm[:, h:h + 1], ps_c1[:]),
                           r="anm", w=("anegm", "anm"))
                P.step("vector", lambda E: E.engine_nop(), r=("aqk", "avtm", "anegm"),
                       w=("a_flqk", "a_flvtm", "a_flnegm", "aqk", "avtm"))
            if ATTN_SUB == "proj":
                return
            with P.scope():
                def kf_head(h):
                    return [(lambda kt, h=h: kf[(h % 2) * 64:(h % 2) * 64 + 64, h // 2,
                                                kt * 128:(kt + 1) * 128], None)]

                def qf_head(h, qb):
                    return [qf[(h % 2) * 64:(h % 2) * 64 + 64, h // 2, qb * 512:(qb + 1) * 512]]

                def vtm_slice(h, kt):
                    return vtm[:, kt * 1040 + h * 65: kt * 1040 + (h + 1) * 65]

                def negm_col(h):
                    return negm[:, h:h + 1]

                def out_write(h, qb, onorm, key):
                    c, half = h // 2, h % 2
                    P.step("vector", lambda E, c=c, half=half, qb=qb, onorm=onorm: E.tensor_copy(
                        af[half * 64:half * 64 + 64, c, qb * 512:(qb + 1) * 512], onorm[:]),
                        r=key, w=("aaf", key))

                flash_attn_T(P, ctx, "a_fl", b, range(FLASH_NH), kf_head, qf_head, vtm_slice,
                             negm_col, out_write, n_ktiles=16, causal=False)
            if ATTN_SUB == "flash":
                return
            if "qf_dbg" in io:
                for c in range(DC):
                    P.dma("sync", io["qf_dbg"][b, c, :, :], qf[:, c, :], r="a_flqk", w="qfd")
                    P.dma("sync", io["kf_dbg"][b, c, :, :], kf[:, c, :], r="a_flqk", w="qfd")
                P.dma("sync", io["vt_dbg"][b, :, :], vtm[:, :], r="a_flvtm", w="qfd")
            if "af_dbg" in io:
                for c in range(DC):
                    P.dma("sync", io["af_dbg"][b, c, :, :], af[:, c, :], r="aaf", w="afd")
            with P.scope():
                wo = P.sbuf("a_wo", [128, DC, 1024], BF16)
                for c in range(DC):
                    P.dma("gpsimd", wo[:, c, :], io["aw_o"][c * 128:(c + 1) * 128, :], w="awo")
                bo = P.sbuf("a_bo", [128, 8], F32)
                P.dma("sync", bo[:], io["ab_o"][:, :], w="abo")
                s0t = P.sbuf("a_s0t", [128, 512], F32)
                ps = [P.psum(f"a_ps{i}", [128, 512]) for i in range(2)]
                for tb in range(T2 // 512):
                    tsl = slice(tb * 512, (tb + 1) * 512)
                    gsl = slice(b * T2 + tb * 512, b * T2 + (tb + 1) * 512)
                    for oc in range(8):
                        pb = ps[oc % 2]
                        pk = f"ap{oc % 2}"
                        for c in range(DC):
                            P.step("tensor", lambda E, c=c, oc=oc, pb=pb, tsl=tsl: E.matmul(
                                pb[:], wo[:, c, oc * 128:(oc + 1) * 128], af[:, c, tsl],
                                start=(c == 0), stop=(c == DC - 1)),
                                r=("awo", "aaf", pk), w=pk)
                        P.step("scalar", lambda E, oc=oc, pb=pb: E.activation(
                            out=s0t[:], in_=pb[:], func=AF.Identity, bias=bo[:, oc:oc + 1]),
                            r=(pk, "abo"), w=("as0", pk))
                        P.dma("sync", io["s0"][oc, :, gsl], s0t[:], r="as0", w="as0d")

    if "attn" in phases:
        for b in range(NB):
            _l0_attn(b)

    if "ffnattn" in phases or "moe" in phases:
        io["_mrows_sb"] = P.sbuf("mrows", [8, T], BF16)
    def _l0_ffnattn(b):
        import os as _os
        mrows = io["_mrows_sb"]
        with P.scope():
            qf = P.sbuf("f_qf", [128, DC, T2], BF16)
            kf = P.sbuf("f_kf", [128, 4, T2], BF16)
            vtm = P.sbuf("f_vtm", [128, 16 * 4 * 65], BF16)
            af = P.sbuf("f_af", [128, DC, T2], BF16)
            negm = P.sbuf("f_negm", [128, 16], F32)
            wg = P.sbuf("f_wg", [64, 64], BF16)
            P.dma("gpsimd", wg[:], io["wg"][:, :], w="fwg")
            bg = P.sbuf("f_bg", [64, 1], F32)
            P.dma("sync", bg[:], io["bg"][:, :], w="fbg")
            with P.scope():
                wfq = P.sbuf("f_wq", [128, DC, 1024], BF16)
                wfk = P.sbuf("f_wk", [128, DC, 512], BF16)
                wfv = P.sbuf("f_wv", [128, DC, 256], BF16)
                for c in range(DC):
                    P.dma("gpsimd", wfq[:, c, :], io["fw_q"][c * 128:(c + 1) * 128, :], w="fwq")
                    P.dma("gpsimd", wfk[:, c, :], io["fw_k"][c * 128:(c + 1) * 128, :], w="fwk")
                    P.dma("gpsimd", wfv[:, c, :], io["fw_v"][c * 128:(c + 1) * 128, :], w="fwv")
                fbq = P.sbuf("f_bq", [128, 8], F32)
                P.dma("sync", fbq[:], io["fb_q"][:, :], w="fbq")
                fbk = P.sbuf("f_bk", [128, 4], F32)
                P.dma("sync", fbk[:], io["fb_k"][:, :], w="fbk")
                fbv = P.sbuf("f_bv", [128, 260], F32)
                P.dma("sync", fbv[:], io["fb_vrep"][:, :], w="fbv")
                hb = [P.sbuf(f"f_hb{c}", [128, 512], BF16) for c in range(DC)]
                ps = [P.psum(f"f_ps{i}", [128, 512]) for i in range(2)]
                for tb in range(T2 // 512):
                    tsl = slice(tb * 512, (tb + 1) * 512)
                    gsl = slice(b * T2 + tb * 512, b * T2 + (tb + 1) * 512)
                    for c in range(DC):
                        P.dma("sync", hb[c][:], io["h"][c, :, gsl], w=f"fh{c}")
                    for oc in range(12):
                        pb = ps[oc % 2]
                        pk = f"fp{oc % 2}"
                        wsb = wfq if oc < 8 else wfk
                        col = (oc % 8) * 128
                        for c in range(DC):
                            P.step("tensor", lambda E, c=c, col=col, pb=pb, tb=tb, wsb=wsb: E.matmul(
                                pb[:], wsb[:, c, col:col + 128], hb[c][:],
                                start=(c == 0), stop=(c == DC - 1)),
                                r=("fwq", "fwk", f"fh{c}", pk), w=pk)
                        dst = qf if oc < 8 else kf
                        dc_ = oc % 8
                        bias = fbq if oc < 8 else fbk
                        P.step("scalar", lambda E, dst=dst, dc_=dc_, pb=pb, tsl=tsl, bias=bias:
                               E.activation(out=dst[:, dc_, tsl], in_=pb[:], func=AF.Identity,
                               bias=bias[:, dc_:dc_ + 1]), r=(pk, "fbq", "fbk"), w=("fqk", pk))
                    for st_ in range(4):
                        tile = tb * 4 + st_
                        pb = ps[st_ % 2]
                        pk = f"fp{st_ % 2}"
                        for c in range(DC):
                            P.step("tensor", lambda E, c=c, st_=st_, pb=pb: E.matmul(
                                pb[:, 0:256], hb[c][:, st_ * 128:(st_ + 1) * 128],
                                wfv[:, c, :], start=(c == 0), stop=(c == DC - 1)),
                                r=("fwv", f"fh{c}", pk), w=pk)
                        P.step("vector", lambda E, tile=tile, pb=pb: E.tensor_tensor(
                            out=vtm[:, tile * 260: (tile + 1) * 260]
                                .rearrange("p (h x) -> p h x", x=65)[:, :, 0:64],
                            in0=pb[:, 0:256].rearrange("p (h x) -> p h x", x=64),
                            in1=fbv[:, :].rearrange("p (h x) -> p h x", x=65)[:, :, 0:64],
                            op=ALU.add), r=(pk, "fbv"), w=("fvtm", pk))
                P.step("vector", lambda E: E.memset(
                    vtm[:].rearrange("p (t x) -> p t x", x=65)[:, :, 64:65], 1.0), w="fvtm")
                cos2 = P.sbuf("f_cos", [64, T2], F32)
                sin2 = P.sbuf("f_sin", [64, T2], F32)
                P.dma("sync", cos2[:], io["cos2"][:, :], w="fcos")
                P.dma("sync", sin2[:], io["sin2"][:, :], w="fsin")
                rt = [P.sbuf(f"f_rt{i}", [64, T2], BF16) for i in range(3)]
                for which, nchunk, src in (("q", DC, qf), ("k", 4, kf)):
                    for c in range(nchunk):
                        P.step("vector", lambda E, src=src, c=c: E.tensor_copy(
                            rt[2][:], src[64:128, c, :]), r=("fqk", "frope"), w="fr2")
                        P.step("vector", lambda E, src=src, c=c: E.tensor_tensor(
                            out=rt[0][:], in0=src[0:64, c, :], in1=sin2[:], op=ALU.mult),
                            r=("fqk", "fsin", "frope"), w="fr0")
                        P.step("vector", lambda E: E.tensor_tensor(
                            out=rt[1][:], in0=rt[2][:], in1=cos2[:], op=ALU.mult),
                            r=("fr2", "fcos", "frope"), w="fr1")
                        P.step("vector", lambda E: E.tensor_tensor(
                            out=rt[1][:], in0=rt[1][:], in1=rt[0][:], op=ALU.add),
                            r=("fr0", "fr1"), w="fr1")
                        P.step("vector", lambda E: E.tensor_tensor(
                            out=rt[0][:], in0=rt[2][:], in1=sin2[:], op=ALU.mult),
                            r=("fr2", "fsin"), w="fr0")
                        P.step("vector", lambda E, src=src, c=c: E.tensor_tensor(
                            out=src[0:64, c, :], in0=src[0:64, c, :], in1=cos2[:], op=ALU.mult),
                            r=("fqk", "fcos"), w="fqk")
                        P.step("vector", lambda E, src=src, c=c: E.tensor_sub(
                            src[0:64, c, :], src[0:64, c, :], rt[0][:]),
                            r=("fqk", "fr0"), w="fqk")
                        P.step("vector", lambda E, src=src, c=c: E.tensor_copy(
                            src[64:128, c, :], rt[1][:]), r=("fr1", "fqk"), w=("fqk", "frope"))
                pq = P.sbuf("f_pq", [128, 128], BF16)
                P.dma("gpsimd", pq[:], io["pq"][:, :], w="fpq")
                pk = P.sbuf("f_pk", [128, 128], BF16)
                P.dma("gpsimd", pk[:], io["pk"][:, :], w="fpk")
                ps_rp = P.psum("f_ps_rp", [128, 512])
                for which, nchunk, src_, pkey, perm in (("q", DC, qf, "fpq", pq),
                                                        ("k", 4, kf, "fpk", pk)):
                    for c in range(nchunk):
                        for nb_ in range(4):
                            P.step("tensor", lambda E, src_=src_, c=c, nb_=nb_, perm=perm:
                                   E.matmul(ps_rp[:], perm[:],
                                   src_[:, c, nb_ * 512:(nb_ + 1) * 512],
                                   start=True, stop=True),
                                   r=("fqk", pkey, "frp"), w="frp")
                            P.step("scalar", lambda E, src_=src_, c=c, nb_=nb_: E.activation(
                                   out=src_[:, c, nb_ * 512:(nb_ + 1) * 512], in_=ps_rp[:],
                                   func=AF.Identity), r="frp", w=("fqk", "frp"))
                sq = P.sbuf("f_sq", [128, 2048], BF16)
                ps_n2 = P.psum("f_ps_n2", [16, 2048])
                ps_rb = ps_rp
                self_ = P.sbuf("f_self", [128, DC * 16], BF16)
                P.dma("sync", self_[:], io["selq"][:, :], w="fsel")
                selk = P.sbuf("f_selk", [128, 4 * 16], BF16)
                P.dma("sync", selk[:], io["selkp"][:, :], w="fselk")
                eh2f = P.sbuf("f_eh2f", [16, DC * 128], F32)
                P.dma("sync", eh2f[:], io["eh2p"][:, :], w="feh2f")
                eh2k = P.sbuf("f_eh2k", [16, 4 * 128], F32)
                P.dma("sync", eh2k[:], io["eh2k"][:, :], w="feh2k")
                qnw = P.sbuf("f_qnw", [128, 1], F32)
                P.dma("sync", qnw[:], io["qnwp"][:, :], w="fqnw")
                knw = P.sbuf("f_knw", [128, 1], F32)
                P.dma("sync", knw[:], io["knwp"][:, :], w="fknw")
                rstd = P.sbuf("f_rstd", [16, 2048], F32)
                n2r = P.sbuf("f_n2r", [16, 2], F32)
                mcol = P.sbuf("f_mcol", [16, 1], F32)
                rep4 = P.sbuf("f_rep4", [4, 16], F32)
                P.dma("sync", rep4[:], io["rep4"][:, :], w="frep4")
                ehrep = P.sbuf("f_ehrep", [16, 16 * 128], F32)
                P.dma("sync", ehrep[:], io["ehrep_f"][:, :], w="fehrep")
                ps_c1 = P.psum("f_ps_c1", [128, 1])
                ps_m1 = ps_c1
                for which, nchunk, src, sel, eh2, wcol, wkey in (
                        ("q", DC, qf, self_, eh2f, qnw, "fqnw"),
                        ("k", 4, kf, selk, eh2k, knw, "fknw")):
                    for c in range(nchunk):
                        P.step("scalar", lambda E, src=src, c=c: E.activation(
                            out=sq[:], in_=src[:, c, :], func=AF.Square), r=("fqk", "fsq2"), w="fsq")
                        for nb_ in range(4):
                            P.step("tensor", lambda E, c=c, nb_=nb_, sel=sel, nchunk=nchunk: E.matmul(
                                ps_n2[:, nb_ * 512:(nb_ + 1) * 512],
                                sel[:, c * 16:(c + 1) * 16], sq[:, nb_ * 512:(nb_ + 1) * 512],
                                start=(c == 0), stop=(c == nchunk - 1)),
                                r=("fsq", "fsel", "fselk", "fn2"), w=("fn2", "fsq2"))
                    P.step("vector", lambda E: E.tensor_scalar(out=rstd[:], in0=ps_n2[:],
                           scalar1=1.0 / 64, scalar2=1e-6, op0=ALU.mult, op1=ALU.add),
                           r="fn2", w=("frstd", "fn2"))
                    P.step("scalar", lambda E: E.activation(out=rstd[:], in_=rstd[:], func=AF.Sqrt),
                           r="frstd", w="frstd")
                    P.step("vector", lambda E: E.reciprocal(rstd[:], rstd[:]), r="frstd", w="frstd")
                    for c in range(nchunk):
                        for nb_ in range(4):
                            P.step("tensor", lambda E, c=c, nb_=nb_, eh2=eh2: E.matmul(
                                ps_rb[:], eh2[:, c * 128:(c + 1) * 128],
                                rstd[:, nb_ * 512:(nb_ + 1) * 512], start=True, stop=True),
                                r=("frstd", "feh2f", "feh2k", "frp"), w="frp")
                            P.step("vector", lambda E, src=src, c=c, nb_=nb_, wcol=wcol:
                                   E.scalar_tensor_tensor(
                                   out=src[:, c, nb_ * 512:(nb_ + 1) * 512],
                                   in0=src[:, c, nb_ * 512:(nb_ + 1) * 512],
                                   scalar=wcol[:, 0:1], in1=ps_rb[:],
                                   op0=ALU.mult, op1=ALU.mult),
                                   r=("fqk", "frp", wkey), w=("fqk", "frp"))
                    for c in range(nchunk):
                        P.step("scalar", lambda E, src=src, c=c: E.activation(
                            out=sq[:], in_=src[:, c, :], func=AF.Square), r=("fqk", "fsq2"), w="fsq")
                        for nb_ in range(4):
                            P.step("tensor", lambda E, c=c, nb_=nb_, sel=sel, nchunk=nchunk: E.matmul(
                                ps_n2[:, nb_ * 512:(nb_ + 1) * 512],
                                sel[:, c * 16:(c + 1) * 16], sq[:, nb_ * 512:(nb_ + 1) * 512],
                                start=(c == 0), stop=(c == nchunk - 1)),
                                r=("fsq", "fsel", "fselk", "fn2"), w=("fn2", "fsq2"))
                    ci = 0 if which == "q" else 1
                    P.step("vector", lambda E, ci=ci: E.tensor_reduce(
                        out=n2r[:, ci:ci + 1], in_=ps_n2[:], axis=AX.X, op=ALU.max),
                        r="fn2", w=("fn2r", "fn2"))
                P.step("scalar", lambda E: E.activation(out=n2r[:], in_=n2r[:], func=AF.Sqrt),
                       r="fn2r", w="fn2r")
                P.step("tensor", lambda E: E.matmul(ps_m1[0:16, 0:1], rep4[0:4, :], n2r[0:4, 1:2],
                       start=True, stop=True), r=("fn2r", "frep4", "fnm"), w="fm1")
                P.step("vector", lambda E: E.tensor_tensor(out=mcol[:], in0=n2r[:, 0:1],
                       in1=ps_m1[0:16, 0:1], op=ALU.mult), r=("fn2r", "fm1"), w=("fmc0", "fm1"))
                P.step("vector", lambda E: E.tensor_scalar(out=mcol[:], in0=mcol[:],
                       scalar1=-0.125, scalar2=None, op0=ALU.mult), r="fmc0", w="fmc")
                for h in range(16):
                    P.step("tensor", lambda E, h=h: E.matmul(ps_c1[:], ehrep[:, h * 128:(h + 1) * 128],
                           mcol[:], start=True, stop=True), r=("fmc", "fehrep", "fnm"), w="fnm")
                    P.step("vector", lambda E, h=h: E.tensor_copy(negm[:, h:h + 1], ps_c1[:]),
                           r="fnm", w=("fnegm", "fnm"))
                P.step("vector", lambda E: E.engine_nop(), r=("fqk", "fvtm", "fnegm"),
                       w=("f_flqk", "f_flvtm", "f_flnegm"))
            if FFN_SUB == "proj":
                return
            with P.scope():
                maskc = P.sbuf("f_maskc", [128, 896], BF16)
                P.dma("sync", maskc[:], io["maskc"][:, :], w="maskc")
                ps_g = P.psum("f_ps_g", [64, 512])
                gsb = P.sbuf("f_gsb", [64, 512], BF16)

                def kf_head(h):
                    kvh = h // 4
                    bb = (h % 2) * 64
                    return [(lambda kt, kvh=kvh, bb=bb: kf[bb:bb + 64, kvh,
                             kt * 128:(kt + 1) * 128], None)]

                def qf_head(h, qb):
                    c = h // 2
                    bb = (h % 2) * 64
                    return [qf[bb:bb + 64, c, qb * 512:(qb + 1) * 512]]

                def vtm_slice(h, kt):
                    kvh = h // 4
                    return vtm[:, kt * 260 + kvh * 65: kt * 260 + (kvh + 1) * 65]

                def negm_col(h):
                    return negm[:, h:h + 1]

                def out_write(h, qb, onorm, key):
                    c, half = h // 2, h % 2
                    P.step("tensor", lambda E, onorm=onorm: E.matmul(
                        ps_g[:], wg[:], onorm[:], start=True, stop=True),
                        r=(key, "fwg", "fpg"), w="fpg")
                    P.step("scalar", lambda E: E.activation(out=gsb[:], in_=ps_g[:],
                           func=AF.Sigmoid, bias=bg[:, 0:1]), r=("fpg", "fbg"), w=("fgsb", "fpg"))
                    P.step("vector", lambda E, c=c, half=half, qb=qb, onorm=onorm: E.tensor_tensor(
                        out=af[half * 64:half * 64 + 64, c, qb * 512:(qb + 1) * 512],
                        in0=onorm[:], in1=gsb[:], op=ALU.mult),
                        r=(key, "fgsb"), w=("faf", key))

                flash_attn_T(P, ctx, "f_fl", b, range(16), kf_head, qf_head, vtm_slice,
                             negm_col, out_write, n_ktiles=16, causal=True, maskc=maskc,
                             ksplit=True)
            if FFN_SUB == "flash":
                return
            with P.scope():
                wo = P.sbuf("f_wo", [128, DC, 1024], BF16)
                for c in range(DC):
                    P.dma("gpsimd", wo[:, c, :], io["fw_o"][c * 128:(c + 1) * 128, :], w="fwo")
                fbo = P.sbuf("f_bo", [128, 8], F32)
                P.dma("sync", fbo[:], io["fb_o"][:, :], w="fbo")
                ln2w = P.sbuf("f_ln2w", [128, 8], F32)
                P.dma("sync", ln2w[:], io["ln2w"][:, :], w="fln2w")
                wr = P.sbuf("f_wr", [128, DC, 8], BF16)
                for c in range(DC):
                    P.dma("gpsimd", wr[:, c, :], io["wr"][c * 128:(c + 1) * 128, :], w="fwr")
                brrep = P.sbuf("f_brrep", [128, 8], F32)
                P.dma("sync", brrep[:], io["brrep"][:, :], w="fbrrep")
                id128b = P.sbuf("f_id", [128, 128], BF16)
                P.dma("sync", id128b[:], io["id128b"][:, :], w="fid")
                xnb = [P.sbuf(f"f_xn{c}", [128, 512], F32) for c in range(DC)]
                xmid = [P.sbuf(f"f_xm{c}", [128, 512], F32) for c in range(DC)]
                h2b = [P.sbuf(f"f_h2{c}", [128, 512], BF16) for c in range(DC)]
                sqf = P.sbuf("f_sqf", [128, 512], F32)
                rr = P.sbuf("f_rr", [1, 512], F32)
                lg = P.sbuf("f_lg", [128, 8], F32)
                lgm = P.sbuf("f_lgm", [128, 1], F32)
                mk_ = P.sbuf("f_mk", [128, 8], BF16)
                ps = [P.psum(f"f_ps{i}", [128, 512]) for i in range(2)]
                ps_r = P.psum("f_ps_r", [1, 512])
                ps_rb2 = P.psum("f_ps_rb2", [128, 512])
                ps_l = P.psum("f_ps_l", [128, 8])
                ps_mt = P.psum("f_ps_mt", [8, 128], F32)
                for tb in range(T2 // 512):
                    tsl = slice(tb * 512, (tb + 1) * 512)
                    gsl = slice(b * T2 + tb * 512, b * T2 + (tb + 1) * 512)
                    for c in range(DC):
                        P.dma("sync", xnb[c][:], io["xn32"][c, :, gsl], w=f"fxn{c}")
                    for oc in range(8):
                        pb = ps[oc % 2]
                        pk = f"fp{oc % 2}"
                        for c in range(DC):
                            P.step("tensor", lambda E, c=c, oc=oc, pb=pb, tsl=tsl: E.matmul(
                                pb[:], wo[:, c, oc * 128:(oc + 1) * 128], af[:, c, tsl],
                                start=(c == 0), stop=(c == DC - 1)),
                                r=("fwo", "faf", pk), w=pk)
                        P.step("scalar", lambda E, oc=oc, pb=pb: E.activation(
                            out=xmid[oc][:], in_=pb[:], func=AF.Identity, bias=fbo[:, oc:oc + 1]),
                            r=(pk, "fbo"), w=(f"fxm{oc}", pk))
                        P.step("vector", lambda E, oc=oc: E.tensor_add(
                            xmid[oc][:], xmid[oc][:], xnb[oc][:]),
                            r=(f"fxm{oc}", f"fxn{oc}"), w=f"fxm{oc}")
                        P.dma("sync", io["xmid"][oc, :, gsl], xmid[oc][:], r=f"fxm{oc}", w="fxmd")
                        if FFN_SUB == "mid0":
                            continue
                        P.step("scalar", lambda E, oc=oc: E.activation(
                            out=sqf[:], in_=xmid[oc][:], func=AF.Square), r=(f"fxm{oc}", "fsqf2"),
                            w="fsqf")
                        P.step("tensor", lambda E, oc=oc: E.matmul(ps_r[:], ctx.ones128_f[:], sqf[:],
                               start=(oc == 0), stop=(oc == 7)), r=("fsqf", "fpsr", "c0"),
                               w=("fpsr", "fsqf2"))
                    if FFN_SUB == "mid0":
                        continue
                    P.step("vector", lambda E: E.tensor_scalar(out=rr[:], in0=ps_r[:],
                           scalar1=1.0 / D, scalar2=1e-6, op0=ALU.mult, op1=ALU.add),
                           r="fpsr", w=("frr", "fpsr"))
                    P.step("scalar", lambda E: E.activation(out=rr[:], in_=rr[:], func=AF.Sqrt),
                           r="frr", w="frr")
                    P.step("vector", lambda E: E.reciprocal(rr[:], rr[:]), r="frr", w="frr")
                    P.step("tensor", lambda E: E.matmul(ps_rb2[:], ctx.onesrow_f[:], rr[:],
                           start=True, stop=True), r=("frr", "fprb2", "c2"), w="fprb2")
                    for c in range(DC):
                        P.step("vector", lambda E, c=c: E.scalar_tensor_tensor(
                            out=h2b[c][:], in0=xmid[c][:], scalar=ln2w[:, c:c + 1], in1=ps_rb2[:],
                            op0=ALU.mult, op1=ALU.mult),
                            r=(f"fxm{c}", "fprb2", "fln2w"), w=f"fh2{c}")
                        P.dma("sync", io["h2"][c, :, gsl], h2b[c][:], r=f"fh2{c}", w="fh2d")
                    for st_ in (range(0) if FFN_SUB == "mid" else range(4)):
                        for c in range(DC):
                            P.step("tensor", lambda E, c=c, st_=st_: E.matmul(
                                ps_l[:], h2b[c][:, st_ * 128:(st_ + 1) * 128], wr[:, c, :],
                                start=(c == 0), stop=(c == DC - 1)),
                                r=(f"fh2{c}", "fwr", "fpl"), w="fpl")
                        P.step("vector", lambda E: E.tensor_tensor(out=lg[:], in0=ps_l[:],
                               in1=brrep[:], op=ALU.add), r=("fpl", "fbrrep"), w=("flg", "fpl"))
                        P.step("vector", lambda E: E.tensor_reduce(out=lgm[:], in_=lg[:],
                               axis=AX.X, op=ALU.max), r="flg", w="flgm")
                        P.step("vector", lambda E: E.tensor_scalar(out=mk_[:], in0=lg[:],
                               scalar1=lgm[:, 0:1], scalar2=None, op0=ALU.is_equal),
                               r=("flg", "flgm"), w="fmk")
                        if FFN_SUB != "route1":
                            P.step("tensor", lambda E: E.matmul(ps_mt[:], mk_[:], id128b[:],
                                   start=True, stop=True), r=("fmk", "fid", "fmt"), w="fmt")
                            if FFN_SUB != "route2":
                                src_ap = (mk_[0:8, 0:128] if FFN_SUB == "route3" else ps_mt[:])
                                P.step("scalar", lambda E, tb=tb, st_=st_, b=b, src_ap=src_ap: E.activation(
                                    out=mrows[:, b * T2 + tb * 512 + st_ * 128: b * T2 + tb * 512 + (st_ + 1) * 128],
                                    in_=src_ap, func=AF.Identity), r="fmt", w=("fmrows", "fmt"))

    if "ffnattn" in phases:
        for b in range(NB):
            _l0_ffnattn(b)

    def _l0_moe():
      mrows = io["_mrows_sb"]
      with P.scope():
        w1 = P.sbuf("m_w1", [128, DC, 4096], BF16)
        w2 = P.sbuf("m_w2", [128, 16, 1024], BF16)
        b1c = P.sbuf("m_b1c", [128, 32], F32)
        b2sb = P.sbuf("m_b2", [8, 1024], BF16)
        P.dma("gpsimd", b2sb[:], io["b2sb"][:, :], w="mb2")
        erow = P.sbuf("m_erow", [8, 8 * 128], BF16)
        P.dma("sync", erow[:], io["erow"][:, :], w="merow")
        h2b = [P.sbuf(f"m_h2{c}", [128, 512], BF16) for c in range(DC)]
        act = P.sbuf("m_act", [128, 16, 512], BF16)
        sg = P.sbuf("m_sg", [128, 512], BF16)
        ysb = P.sbuf("m_ysb", [128, 512], F32)
        ps_u = [P.psum(f"m_ps_u{i}", [128, 512]) for i in range(2)]
        ps_y = P.psum("m_ps_y", [128, 512])
        ps_mb = P.psum("m_ps_mb", [128, 512])
        ps_b2 = P.psum("m_ps_b2", [128, 512])
        import os as _os
        for e in range(int(_os.environ.get('MOE_E', '8'))):
            for c in range(DC):
                P.dma("gpsimd", w1[:, c, :], io["w1"][e, c * 128:(c + 1) * 128, :], w=f"mw1_{c}")
            for c in range(16):
                P.dma("gpsimd", w2[:, c, :], io["w2"][e, c * 128:(c + 1) * 128, :], w=f"mw2_{c}")
            P.dma("sync", b1c[:], io["b1c"][e, :, :], w="mb1")
            if _os.environ.get("MOE_PART") == "w":
                continue
            for tb in range(T // 512):
                tsl = slice(tb * 512, (tb + 1) * 512)
                if _os.environ.get("MOE_PART") == "wh":
                    continue
                for c in range(DC):
                    P.dma("sync", h2b[c][:], io["h2"][c, :, tsl], w=f"mh{c}")
                P.step("tensor", lambda E, e=e, tsl=tsl: E.matmul(
                    ps_mb[:], erow[:, e * 128:(e + 1) * 128], mrows[:, tsl],
                    start=True, stop=True), r=("merow", "fmrows", "mpmb"), w="mpmb")
                if _os.environ.get("MOE_PART") == "wm":
                    continue
                for j in range(16):
                    for half in range(2):
                        uc = j + 16 * half
                        pb = ps_u[half]
                        pk = f"mpu{half}"
                        for c in range(DC):
                            P.step("tensor", lambda E, c=c, uc=uc, pb=pb, e=e, tb=tb: E.matmul(
                                pb[:], w1[:, c, uc * 128:(uc + 1) * 128], h2b[c][:],
                                start=(c == 0), stop=(c == DC - 1)),
                                r=(f"mw1_{c}", f"mh{c}", pk), w=pk)
                    P.step("scalar", lambda E, j=j: E.activation(
                        out=sg[:], in_=ps_u[1][:], func=AF.Sigmoid, bias=b1c[:, 16 + j:17 + j]),
                        r=("mpu1", "mb1"), w=("msg", "mpu1"))
                    P.step("vector", lambda E, j=j: E.scalar_tensor_tensor(
                        out=act[:, j, :], in0=ps_u[0][:], scalar=b1c[:, j:j + 1], in1=sg[:],
                        op0=ALU.add, op1=ALU.mult),
                        r=("mpu0", "msg", "mb1"), w=("mact", "mpu0"))
                if _os.environ.get("MOE_PART") == "wmu":
                    continue
                for oc in range(8):
                    for hc in range(16):
                        P.step("tensor", lambda E, hc=hc, oc=oc: E.matmul(
                            ps_y[:], w2[:, hc, oc * 128:(oc + 1) * 128], act[:, hc, :],
                            start=(hc == 0), stop=(hc == 15)),
                            r=(f"mw2_{hc}", "mact", "mpy"), w="mpy")
                    P.step("scalar", lambda E: E.activation(out=ysb[:], in_=ps_y[:],
                           func=AF.Identity), r="mpy", w=("mysb", "mpy"))
                    P.step("vector", lambda E: E.tensor_tensor(out=ysb[:], in0=ysb[:],
                           in1=ps_mb[:], op=ALU.mult), r=("mysb", "mpmb"), w="mysb")
                    if e == 7:
                        P.step("tensor", lambda E, oc=oc, tsl=tsl: E.matmul(
                            ps_b2[:], b2sb[:, oc * 128:(oc + 1) * 128], mrows[:, tsl],
                            start=True, stop=True), r=("mb2", "fmrows", "mpb2"), w="mpb2")
                        P.step("vector", lambda E: E.tensor_tensor(out=ysb[:], in0=ysb[:],
                               in1=ps_b2[:], op=ALU.add), r=("mysb", "mpb2"), w=("mysb", "mpb2"))
                    if e == 0:
                        P.dma("sync", io["y"][oc, :, tsl], ysb[:], r="mysb", w=f"myd{oc}_{tb}")
                    else:
                        def fn(E, oc=oc, tsl=tsl, y=ysb):
                            return E.dma_start(out=io["y"][oc, :, tsl], in_=y[:],
                                               accum_op=ALU.add)
                        P.step("gpsimd", fn, r=("mysb", f"myd{oc}_{tb}"), w=f"myd{oc}_{tb}",
                               dma=True)

    if "moe" in phases:
        _l0_moe()

    if "combine" in phases:
      import os as _os
      with P.scope():
        if _os.environ.get("CMB_MIN"):
            mot = P.sbuf("cmb_min", [128, 512], F32)
            P.step("vector", lambda E: E.memset(mot[:], 1.0), w="cmbm")
            for c in range(DC):
                for tb in range(T // 512):
                    P.dma("sync", io["x_out"][c, :, tb * 512:(tb + 1) * 512], mot[:],
                          r="cmbm", w="cmbo")
        else:
            _yy = io["s0"] if _os.environ.get("CMB_NOY") else io["y"]
            silu_combine_store(ctx, io["x"], io["s0"], _yy, io["x_out"],
                               extra_add_dram=io["xmid"])


def _to_fm(xbsd):
    B, S_, D_ = xbsd.shape
    return np.ascontiguousarray(xbsd.reshape(B * S_, D_).T.reshape(DC, 128, B * S_))


def _from_fm(xfm, B, S_):
    return np.ascontiguousarray(xfm.reshape(D, B * S_).T.reshape(B, S_, D))


def prep_layer0_consts(z):
    import ml_dtypes
    bf = ml_dtypes.bfloat16
    d = {}
    Wqkv = z["L0_attn_Wqkv"]
    cols = np.zeros((3072,), np.int64)
    for g in range(3):
        for h in range(16):
            for dd in range(64):
                cols[g * 1024 + h * 64 + dd] = h * 192 + g * 64 + dd
    d["aw_qkv"] = np.ascontiguousarray(Wqkv[cols, :].T)
    bq = z["L0_attn_bqkv"][cols]
    d["ab_qk"] = np.ascontiguousarray(bq[:2048].reshape(16, 128).T)
    vrep = np.zeros((128, 1040), np.float32)
    for h in range(16):
        vrep[:, h * 65:h * 65 + 64] = bq[2048 + h * 64: 2048 + (h + 1) * 64][None, :]
    d["ab_vrep"] = vrep
    d["aw_o"] = np.ascontiguousarray(z["L0_attn_Wo"].T)
    d["ab_o"] = np.ascontiguousarray(z["L0_attn_bo"].reshape(8, 128).T)
    d["ln1w"] = np.ascontiguousarray(z["L0_ffn_ln1_w"].reshape(8, 128).T)
    selq = np.zeros((128, DC * 16), np.float32)
    for c in range(DC):
        selq[0:64, c * 16 + 2 * c] = 1
        selq[64:128, c * 16 + 2 * c + 1] = 1
    d["selq"] = selq.astype(bf)
    ehr = np.zeros((16, 16 * 128), np.float32)
    for h in range(16):
        ehr[h, h * 128:(h + 1) * 128] = 1
    d["ehrep"] = ehr
    d["ehrep_f"] = ehr
    def amap(nh_pair_base):
        m = np.zeros(128, np.int64)
        for r in range(128):
            if r < 32:
                m[r] = (nh_pair_base) * 64 + r
            elif r < 64:
                m[r] = (nh_pair_base + 1) * 64 + (r - 32)
            elif r < 96:
                m[r] = (nh_pair_base) * 64 + 32 + (r - 64)
            else:
                m[r] = (nh_pair_base + 1) * 64 + 32 + (r - 96)
        return m
    qcols = np.concatenate([amap(2 * c) for c in range(8)])
    d["fw_q"] = np.ascontiguousarray(z["L0_ffn_Wq"][:, qcols])
    d["fb_q"] = np.ascontiguousarray(z["L0_ffn_bq"][qcols].reshape(8, 128).T)
    def kvmap(kvh):
        m = np.zeros(128, np.int64)
        for r in range(128):
            m[r] = kvh * 64 + (r % 32) + 32 * (r >= 64)
        return m
    kcols = np.concatenate([kvmap(k) for k in range(4)])
    d["fw_k"] = np.ascontiguousarray(z["L0_ffn_Wk"][:, kcols])
    d["fb_k"] = np.ascontiguousarray(z["L0_ffn_bk"][kcols].reshape(4, 128).T)
    d["fw_v"] = z["L0_ffn_Wv"]
    vr = np.zeros((128, 260), np.float32)
    for h in range(4):
        vr[:, h * 65:h * 65 + 64] = z["L0_ffn_bv"][h * 64:(h + 1) * 64][None, :]
    d["fb_vrep"] = vr
    dmap = np.zeros(128, np.int64)
    for r in range(128):
        dmap[r] = (r % 32) + 32 * (r >= 64)
    d["qnw"] = z["L0_ffn_qn_w"][dmap].reshape(128, 1).astype(np.float32)
    d["knw"] = z["L0_ffn_kn_w"][dmap].reshape(128, 1).astype(np.float32)
    inv_freq = 1.0 / 10000.0 ** (np.arange(0, 64, 2, dtype=np.float32) / 64)
    t = np.arange(S, dtype=np.float32)
    fr = np.outer(t, inv_freq)
    cos = np.cos(fr).T
    sin = np.sin(fr).T
    d["cos2"] = np.ascontiguousarray(np.concatenate([cos, cos], 0).astype(np.float32))
    d["sin2"] = np.ascontiguousarray(np.concatenate([sin, sin], 0).astype(np.float32))
    sf = np.zeros((128, DC * 16), np.float32)
    for c in range(DC):
        sf[0:32, c * 16 + 2 * c] = 1
        sf[64:96, c * 16 + 2 * c] = 1
        sf[32:64, c * 16 + 2 * c + 1] = 1
        sf[96:128, c * 16 + 2 * c + 1] = 1
    d["self_"] = sf.astype(bf)
    sk = np.zeros((128, 4 * 16), np.float32)
    for c in range(4):
        sk[0:32, c * 16 + c] = 1
        sk[64:96, c * 16 + c] = 1
    d["selk_"] = sk.astype(bf)
    e2f = np.zeros((16, DC * 128), np.float32)
    for c in range(DC):
        for j in range(128):
            s_ = 1 if (32 <= j < 64 or 96 <= j) else 0
            e2f[2 * c + s_, c * 128 + j] = 1
    d["eh2f"] = e2f
    e2k = np.zeros((16, 4 * 128), np.float32)
    for c in range(4):
        e2k[c, c * 128:(c + 1) * 128] = 1
    d["eh2k"] = e2k
    pq = np.zeros((128, 128), np.float32)
    for r in range(128):
        if r < 32: dst = 0 * 64 + r
        elif r < 64: dst = 64 + (r - 32)
        elif r < 96: dst = 32 + (r - 64)
        else: dst = 64 + 32 + (r - 96)
        pq[r, dst] = 1
    d["pq"] = pq.astype(bf)
    pk = np.zeros((128, 128), np.float32)
    for r in range(128):
        if r < 32: dst = r
        elif r < 64: dst = 64 + (r - 32)
        elif r < 96: dst = 32 + (r - 64)
        else: dst = 96 + (r - 96)
        pk[r, dst] = 1
    d["pk"] = pk.astype(bf)
    e2p = np.zeros((16, DC * 128), np.float32)
    for c in range(DC):
        for j in range(128):
            e2p[2 * c + (j >= 64), c * 128 + j] = 1
    d["eh2p"] = e2p
    dmapp = np.arange(128) % 64
    d["qnwp"] = z["L0_ffn_qn_w"][dmapp].reshape(128, 1).astype(np.float32)
    d["knwp"] = z["L0_ffn_kn_w"][dmapp].reshape(128, 1).astype(np.float32)
    skp = np.zeros((128, 4 * 16), np.float32)
    for c in range(4):
        skp[0:64, c * 16 + c] = 1
    d["selkp"] = skp.astype(bf)
    rep4 = np.zeros((4, 16), np.float32)
    for h in range(16):
        rep4[h // 4, h] = 1
    d["rep4"] = rep4
    mk = np.zeros((128, 896), np.float32)
    for k in range(128):
        for c in range(896):
            if k <= c - 384:
                mk[k, c] = 1
    d["maskc"] = mk.astype(bf)
    d["id128b"] = np.eye(128, dtype=np.float32).astype(bf)
    d["wg"] = z["L0_ffn_Wg"]
    d["bg"] = z["L0_ffn_bg"].reshape(64, 1)
    d["fw_o"] = z["L0_ffn_Wo"]
    d["fb_o"] = np.ascontiguousarray(z["L0_ffn_bo"].reshape(8, 128).T)
    d["ln2w"] = np.ascontiguousarray(z["L0_ffn_ln2_w"].reshape(8, 128).T)
    d["wr"] = z["L0_ffn_Wr"]
    d["brrep"] = np.tile(z["L0_ffn_br"][None, :], (128, 1))
    d["w1"] = z["L0_ffn_W1"]
    b1c = np.zeros((8, 128, 32), np.float32)
    for e in range(8):
        b1c[e] = z["L0_ffn_b1"][e].reshape(32, 128).T
    d["b1c"] = b1c
    d["w2"] = z["L0_ffn_W2"]
    d["b2sb"] = z["L0_ffn_b2"]
    er = np.zeros((8, 8 * 128), np.float32)
    for e in range(8):
        er[e, e * 128:(e + 1) * 128] = 1
    d["erow"] = er.astype(bf)
    return d


def prep_layer1_consts(z):
    import ml_dtypes
    bf = ml_dtypes.bfloat16
    d = {}
    d["l1_wq"] = z["L1_attn_Wq"]
    d["l1_wk"] = z["L1_attn_Wk"]
    d["l1_wv"] = z["L1_attn_Wv"]
    d["l1_wc"] = z["L1_attn_Wc"]
    d["l1_gate"] = np.ascontiguousarray(z["L1_ffn_gate"].T)
    d["l1_cbrep"] = np.tile(z["L1_ffn_corr_bias"][None, :], (128, 1))
    d["l1_w1"] = np.ascontiguousarray(z["L1_ffn_w1"].transpose(0, 2, 1))
    d["l1_w2"] = np.ascontiguousarray(z["L1_ffn_w2"].transpose(0, 2, 1))
    d["l1_fc1"] = np.ascontiguousarray(z["L1_ffn_fc1_w"].T)
    d["l1_fc2"] = np.ascontiguousarray(z["L1_ffn_fc2_w"].T)
    d["l1_up"] = np.ascontiguousarray(z["L1_ffn_up_w"].T)
    d["l1_down"] = np.ascontiguousarray(z["L1_ffn_down_w"].T)
    d["iota8"] = np.tile(np.arange(8, dtype=np.float32)[None, :], (128, 1))
    d["ones8"] = np.ones((128, 8), np.float32)
    d["id128b"] = np.eye(128, dtype=np.float32).astype(bf)
    er = np.zeros((8, 8 * 128), np.float32)
    for e in range(8):
        er[e, e * 128:(e + 1) * 128] = 1
    d["erowf"] = er
    return d


def build_layer1(P, ctx, io):
    BIGV = 1e9
    ln_phase(P, ctx, io["x1"], io["xn1"], pfx="l1ln")

    def _l1_attn(b):
        with P.scope():
            O = P.sbuf("o1", [128, DC, S], BF16)
            with P.scope():
                wq = P.sbuf("w_q", [128, DC, 1024], BF16)
                wk = P.sbuf("w_k", [128, DC, 1024], BF16)
                wv = P.sbuf("w_v", [128, DC, 1024], BF16)
                for c in range(DC):
                    P.dma("gpsimd", wq[:, c, :], io["l1_wq"][c * 128:(c + 1) * 128, :], w="1wq")
                    P.dma("gpsimd", wk[:, c, :], io["l1_wk"][c * 128:(c + 1) * 128, :], w="1wk")
                    P.dma("gpsimd", wv[:, c, :], io["l1_wv"][c * 128:(c + 1) * 128, :], w="1wv")
                xnb = [P.sbuf(f"1x{c}", [128, S], BF16) for c in range(DC)]
                for c in range(DC):
                    P.dma("sync", xnb[c][:], io["xn1"][c, :, b * S:(b + 1) * S], w=f"1x{c}")
                ps = [P.psum(f"1ps{i}", [128, 512]) for i in range(4)]
                Kf = P.sbuf("1K", [128, S], F32)
                Vf = P.sbuf("1V", [128, S], F32)
                Qf = P.sbuf("1Q", [128, S], F32)
                KV = P.sbuf("1KV", [128, S], F32)
                Kc = P.sbuf("1Kc", [128, S], F32)
                for oc in range(DC):
                    for which, wsb, wwk in (("k", wk, "1wk"), ("v", wv, "1wv"), ("q", wq, "1wq")):
                        for nb_ in range(4):
                            pb = ps[nb_]
                            pk = f"1p{nb_}"
                            for c in range(DC):
                                P.step("tensor", lambda E, c=c, oc=oc, nb_=nb_, pb=pb, wsb=wsb:
                                       E.matmul(pb[:], wsb[:, c, oc * 128:(oc + 1) * 128],
                                       xnb[c][:, nb_ * 512:(nb_ + 1) * 512],
                                       start=(c == 0), stop=(c == DC - 1)),
                                       r=(wwk, f"1x{c}", pk), w=pk)
                            if which == "k":
                                P.step("scalar", lambda E, nb_=nb_, pb=pb: E.activation(
                                    out=Kf[:, nb_ * 512:(nb_ + 1) * 512], in_=pb[:], func=AF.Exp),
                                    r=pk, w=("1Kf", pk))
                            elif which == "q":
                                P.step("scalar", lambda E, nb_=nb_, pb=pb: E.activation(
                                    out=Qf[:, nb_ * 512:(nb_ + 1) * 512], in_=pb[:], func=AF.Sigmoid),
                                    r=pk, w=("1Qf", pk))
                            else:
                                P.step("scalar", lambda E, nb_=nb_, pb=pb: E.activation(
                                    out=Vf[:, nb_ * 512:(nb_ + 1) * 512], in_=pb[:], func=AF.Identity),
                                    r=pk, w=("1Vf", pk))
                    P.step("vector", lambda E: E.tensor_mul(KV[:], Kf[:], Vf[:]),
                           r=("1Kf", "1Vf"), w="1KV")
                    P.step("vector", lambda E: E.tensor_tensor_scan(
                        out=Kc[:], data0=ctx.ones128_f[:].broadcast_to((128, S)), data1=Kf[:],
                        initial=0.0, op0=ALU.mult, op1=ALU.add), r=("1Kf", "c0", "1Kc2"), w="1Kc")
                    P.step("vector", lambda E: E.tensor_tensor_scan(
                        out=KV[:], data0=ctx.ones128_f[:].broadcast_to((128, S)), data1=KV[:],
                        initial=0.0, op0=ALU.mult, op1=ALU.add), r=("1KV", "c0"), w="1KV")
                    P.step("vector", lambda E: E.tensor_scalar(out=Kc[:], in0=Kc[:], scalar1=1e-6,
                           scalar2=None, op0=ALU.add), r="1Kc", w="1Kc")
                    P.step("vector", lambda E: E.reciprocal(Kc[:], Kc[:]), r="1Kc", w="1Kc")
                    P.step("vector", lambda E: E.tensor_mul(KV[:], KV[:], Kc[:]),
                           r=("1KV", "1Kc"), w=("1KV", "1Kc2"))
                    P.step("vector", lambda E, oc=oc: E.tensor_tensor(
                        out=O[:, oc, :], in0=KV[:], in1=Qf[:], op=ALU.mult),
                        r=("1KV", "1Qf"), w=("1O", "1KV2"))
            with P.scope():
                wc = P.sbuf("w_c", [128, DC, 1024], BF16)
                for c in range(DC):
                    P.dma("gpsimd", wc[:, c, :], io["l1_wc"][c * 128:(c + 1) * 128, :], w="1wc")
                ps = [P.psum(f"1cps{i}", [128, 512]) for i in range(2)]
                st = P.sbuf("1st", [128, 512], F32)
                for tb in range(S // 512):
                    for oc in range(DC):
                        pb = ps[oc % 2]
                        pk = f"1cp{oc % 2}"
                        for c in range(DC):
                            P.step("tensor", lambda E, c=c, oc=oc, tb=tb, pb=pb: E.matmul(
                                pb[:], wc[:, c, oc * 128:(oc + 1) * 128],
                                O[:, c, tb * 512:(tb + 1) * 512],
                                start=(c == 0), stop=(c == DC - 1)),
                                r=("1wc", "1O", pk), w=pk)
                        P.step("scalar", lambda E, pb=pb: E.activation(out=st[:], in_=pb[:],
                               func=AF.Identity), r=pk, w=("1st", pk))
                        P.dma("sync", io["s1"][oc, :, b * S + tb * 512: b * S + (tb + 1) * 512],
                              st[:], r="1st", w="1std")
    for b in range(NB):
        _l1_attn(b)

    def _l1_ffn():
        crows = P.sbuf("1crows", [8, T], F32)
        with P.scope():
            gate = P.sbuf("1gate", [128, DC, 8], BF16)
            for c in range(DC):
                P.dma("gpsimd", gate[:, c, :], io["l1_gate"][c * 128:(c + 1) * 128, :], w="1g")
            cbrep = P.sbuf("1cb", [128, 8], F32)
            P.dma("sync", cbrep[:], io["l1_cbrep"][:, :], w="1cb")
            iota8 = P.sbuf("1io8", [128, 8], F32)
            P.dma("sync", iota8[:], io["iota8"][:, :], w="1io")
            ones8 = P.sbuf("1on8", [128, 8], F32)
            P.dma("sync", ones8[:], io["ones8"][:, :], w="1on")
            idb = P.sbuf("1idb", [128, 128], BF16)
            P.dma("sync", idb[:], io["id128b"][:, :], w="1id")
            xnbA = [P.sbuf(f"1fx{c}", [128, 512], BF16) for c in range(DC)]
            ps_l = P.psum("1psl", [128, 8])
            ps_t = P.psum("1pst", [8, 128])
            sc = P.sbuf("1sc", [128, 8], F32)
            sfc = P.sbuf("1sfc", [128, 8], F32)
            tmp = P.sbuf("1tmp", [128, 8], F32)
            tmp2 = P.sbuf("1tmp2", [128, 8], F32)
            eq1 = P.sbuf("1eq1", [128, 8], F32)
            eq2 = P.sbuf("1eq2", [128, 8], F32)
            coef = P.sbuf("1coef", [128, 8], BF16)
            m1 = P.sbuf("1m1", [128, 1], F32)
            g0 = P.sbuf("1g0", [128, 1], F32)
            g1 = P.sbuf("1g1", [128, 1], F32)
            wg0 = P.sbuf("1wg0", [128, 1], F32)
            i1 = P.sbuf("1i1", [128, 1], F32)
            i2 = P.sbuf("1i2", [128, 1], F32)
            s1_ = P.sbuf("1s1", [128, 1], F32)
            s2_ = P.sbuf("1s2", [128, 1], F32)
            for tb in range(T // 512):
                for c in range(DC):
                    P.dma("sync", xnbA[c][:], io["xn1"][c, :, tb * 512:(tb + 1) * 512], w=f"1fx{c}")
                for st_ in range(4):
                    for c in range(DC):
                        P.step("tensor", lambda E, c=c, st_=st_: E.matmul(
                            ps_l[:], xnbA[c][:, st_ * 128:(st_ + 1) * 128], gate[:, c, :],
                            start=(c == 0), stop=(c == DC - 1)),
                            r=(f"1fx{c}", "1g", "1pl"), w="1pl")
                    P.step("scalar", lambda E: E.activation(out=sc[:], in_=ps_l[:],
                           func=AF.Sigmoid), r="1pl", w=("1sc", "1pl"))
                    P.step("vector", lambda E: E.tensor_add(sfc[:], sc[:], cbrep[:]),
                           r=("1sc", "1cb"), w="1sfc")
                    for gi, gout in ((0, g0), (1, g1)):
                        sl = slice(gi * 4, gi * 4 + 4)
                        P.step("vector", lambda E, sl=sl, gout=gout: E.tensor_reduce(
                            out=gout[:], in_=sfc[:, sl], axis=AX.X, op=ALU.max),
                            r="1sfc", w=f"1gm{gi}")
                        P.step("vector", lambda E, sl=sl, gout=gout: E.tensor_scalar(
                            out=tmp[:, sl], in0=sfc[:, sl], scalar1=gout[:, 0:1], scalar2=None,
                            op0=ALU.is_equal), r=("1sfc", f"1gm{gi}"), w="1tq")
                        P.step("vector", lambda E, sl=sl: E.scalar_tensor_tensor(
                            out=tmp[:, sl], in0=tmp[:, sl], scalar=-BIGV, in1=sfc[:, sl],
                            op0=ALU.mult, op1=ALU.add), r=("1tq", "1sfc"), w="1tq")
                        P.step("vector", lambda E, sl=sl, gout=gout: E.tensor_reduce(
                            out=tmp2[:, 0:1], in_=tmp[:, sl], axis=AX.X, op=ALU.max),
                            r="1tq", w="1t2")
                        P.step("vector", lambda E, gout=gout: E.tensor_add(
                            gout[:], gout[:], tmp2[:, 0:1]), r=(f"1gm{gi}", "1t2"),
                            w=f"1gm{gi}")
                    P.step("vector", lambda E: E.tensor_tensor(out=wg0[:], in0=g0[:], in1=g1[:],
                           op=ALU.is_ge), r=("1gm0", "1gm1"), w="1wg0")
                    P.step("vector", lambda E: E.tensor_scalar(out=tmp[:, 0:4],
                           in0=ones8[:, 0:4], scalar1=wg0[:, 0:1], scalar2=None, op0=ALU.mult),
                           r=("1on", "1wg0"), w="1tq")
                    P.step("vector", lambda E: E.tensor_scalar(out=s1_[:], in0=wg0[:],
                           scalar1=-1.0, scalar2=1.0, op0=ALU.mult, op1=ALU.add),
                           r="1wg0", w="1nw")
                    P.step("vector", lambda E: E.tensor_scalar(out=tmp[:, 4:8],
                           in0=ones8[:, 4:8], scalar1=s1_[:, 0:1], scalar2=None, op0=ALU.mult),
                           r=("1on", "1nw"), w="1tq")
                    P.step("vector", lambda E: E.tensor_mul(sfc[:], sfc[:], tmp[:]),
                           r=("1sfc", "1tq"), w="1sfc")
                    for rnd, (ii, ssc, eqv) in enumerate(((i1, s1_, eq1), (i2, s2_, eq2))):
                        P.step("vector", lambda E: E.tensor_reduce(out=m1[:], in_=sfc[:],
                               axis=AX.X, op=ALU.max), r="1sfc", w="1m1")
                        P.step("vector", lambda E: E.tensor_scalar(out=tmp[:], in0=sfc[:],
                               scalar1=m1[:, 0:1], scalar2=None, op0=ALU.is_equal),
                               r=("1sfc", "1m1"), w="1tq")
                        P.step("vector", lambda E: E.tensor_scalar(out=tmp2[:], in0=tmp[:],
                               scalar1=-1.0, scalar2=1.0, op0=ALU.mult, op1=ALU.add),
                               r="1tq", w="1t2")
                        P.step("vector", lambda E: E.scalar_tensor_tensor(out=tmp2[:],
                               in0=tmp2[:], scalar=BIGV, in1=iota8[:], op0=ALU.mult,
                               op1=ALU.add), r=("1t2", "1io"), w="1t2")
                        P.step("vector", lambda E, ii=ii: E.tensor_reduce(out=ii[:],
                               in_=tmp2[:], axis=AX.X, op=ALU.min), r="1t2", w="1ii")
                        P.step("vector", lambda E, ii=ii, eqv=eqv: E.tensor_scalar(
                               out=eqv[:], in0=iota8[:], scalar1=ii[:, 0:1], scalar2=None,
                               op0=ALU.is_equal), r=("1io", "1ii"), w="1eqv")
                        P.step("vector", lambda E, eqv=eqv: E.tensor_mul(tmp2[:], eqv[:], sc[:]),
                               r=("1eqv", "1sc"), w="1t2")
                        P.step("vector", lambda E, ssc=ssc: E.tensor_reduce(out=ssc[:],
                               in_=tmp2[:], axis=AX.X, op=ALU.add), r="1t2", w="1ssc")
                        if rnd == 0:
                            P.step("vector", lambda E: E.scalar_tensor_tensor(out=sfc[:],
                                   in0=eq1[:], scalar=-BIGV, in1=sfc[:], op0=ALU.mult,
                                   op1=ALU.add), r=("1eqv", "1sfc"), w="1sfc")
                    P.step("vector", lambda E: E.tensor_add(tmp[:, 0:1], s1_[:], s2_[:]),
                           r="1ssc", w="1sw")
                    P.step("vector", lambda E: E.tensor_scalar(out=tmp[:, 0:1], in0=tmp[:, 0:1],
                           scalar1=1e-20, scalar2=None, op0=ALU.add), r="1sw", w="1sw")
                    P.step("vector", lambda E: E.reciprocal(tmp[:, 0:1], tmp[:, 0:1]),
                           r="1sw", w="1sw")
                    P.step("vector", lambda E: E.tensor_scalar(out=s1_[:], in0=s1_[:],
                           scalar1=tmp[:, 0:1], scalar2=None, op0=ALU.mult), r=("1ssc", "1sw"),
                           w="1w1")
                    P.step("vector", lambda E: E.tensor_scalar(out=s2_[:], in0=s2_[:],
                           scalar1=tmp[:, 0:1], scalar2=None, op0=ALU.mult), r=("1ssc", "1sw"),
                           w="1w2")
                    P.step("vector", lambda E: E.tensor_scalar(out=eq1[:], in0=eq1[:],
                           scalar1=s1_[:, 0:1], scalar2=None, op0=ALU.mult), r=("1eqv", "1w1"),
                           w="1ceq1")
                    P.step("vector", lambda E: E.tensor_scalar(out=eq2[:], in0=eq2[:],
                           scalar1=s2_[:, 0:1], scalar2=None, op0=ALU.mult), r=("1eqv", "1w2"),
                           w="1ceq2")
                    P.step("vector", lambda E: E.tensor_tensor(out=coef[:], in0=eq1[:],
                           in1=eq2[:], op=ALU.add), r=("1ceq1", "1ceq2"), w="1coef")
                    P.step("tensor", lambda E: E.matmul(ps_t[:], coef[:], idb[:],
                           start=True, stop=True), r=("1coef", "1id", "1ptr"), w="1ptr")
                    P.step("scalar", lambda E, tb=tb, st_=st_: E.activation(
                        out=crows[:, tb * 512 + st_ * 128: tb * 512 + (st_ + 1) * 128],
                        in_=ps_t[:], func=AF.Identity), r="1ptr", w=("1crows", "1ptr"))
        with P.scope():
            fc1 = P.sbuf("1fc1", [128, DC, 512], BF16)
            for c in range(DC):
                P.dma("gpsimd", fc1[:, c, :], io["l1_fc1"][c * 128:(c + 1) * 128, :], w="1f1w")
            w1 = P.sbuf("1w1e", [128, 8, 4, 512], BF16)
            w2 = P.sbuf("1w2e", [128, 8, 4, 512], BF16)
            for e in range(8):
                for c in range(4):
                    P.dma("gpsimd", w1[:, e, c, :], io["l1_w1"][e, c * 128:(c + 1) * 128, :],
                          w="1w1e")
                    P.dma("gpsimd", w2[:, e, c, :], io["l1_w2"][e, c * 128:(c + 1) * 128, :],
                          w="1w2e")
            fc2 = P.sbuf("1fc2", [128, 4, 1024], BF16)
            for c in range(4):
                P.dma("gpsimd", fc2[:, c, :], io["l1_fc2"][c * 128:(c + 1) * 128, :], w="1f2w")
            erowf = P.sbuf("1erf", [8, 8 * 128], F32)
            P.dma("sync", erowf[:], io["erowf"][:, :], w="1erf")
            xnbB = [P.sbuf(f"1rx{c}", [128, 512], BF16) for c in range(DC)]
            lat = [P.sbuf(f"1lat{c}", [128, 512], BF16) for c in range(4)]
            t1b = [P.sbuf(f"1t1b{c}", [128, 512], BF16) for c in range(4)]
            comb = [P.sbuf(f"1cmb{c}", [128, 512], F32) for c in range(4)]
            ps = [P.psum(f"1rp{i}", [128, 512]) for i in range(2)]
            ps_cb = P.psum("1pcb", [128, 512])
            cb_sb = P.sbuf("1cbs", [128, 512], F32)
            rtB = P.sbuf("1rt", [128, 512], F32)
            for tb in range(T // 512):
                tsl = slice(tb * 512, (tb + 1) * 512)
                for c in range(DC):
                    P.dma("sync", xnbB[c][:], io["xn1"][c, :, tsl], w=f"1rx{c}")
                for lc in range(4):
                    pb = ps[lc % 2]
                    pk = f"1rp{lc % 2}"
                    for c in range(DC):
                        P.step("tensor", lambda E, c=c, lc=lc, pb=pb: E.matmul(
                            pb[:], fc1[:, c, lc * 128:(lc + 1) * 128], xnbB[c][:],
                            start=(c == 0), stop=(c == DC - 1)),
                            r=("1f1w", f"1rx{c}", pk), w=pk)
                    P.step("scalar", lambda E, lc=lc, pb=pb: E.activation(
                        out=lat[lc][:], in_=pb[:], func=AF.Identity), r=pk, w=("1lat", pk))
                    P.step("vector", lambda E, lc=lc: E.memset(comb[lc][:], 0.0), w="1comb")
                for e in range(8):
                    P.step("tensor", lambda E, e=e, tsl=tsl: E.matmul(
                        ps_cb[:], erowf[:, e * 128:(e + 1) * 128], crows[:, tsl],
                        start=True, stop=True), r=("1erf", "1crows", "1pcb"), w="1pcb")
                    P.step("scalar", lambda E: E.activation(out=cb_sb[:], in_=ps_cb[:],
                           func=AF.Identity), r="1pcb", w=("1cbs", "1pcb"))
                    for ic in range(4):
                        pb = ps[ic % 2]
                        pk = f"1rp{ic % 2}"
                        for c in range(4):
                            P.step("tensor", lambda E, c=c, ic=ic, e=e, pb=pb: E.matmul(
                                pb[:], w1[:, e, c, ic * 128:(ic + 1) * 128], lat[c][:],
                                start=(c == 0), stop=(c == 3)),
                                r=("1w1e", "1lat", pk), w=pk)
                        P.step("vector", lambda E, ic=ic, pb=pb: E.tensor_scalar(
                            out=rtB[:], in0=pb[:], scalar1=0.0, scalar2=None, op0=ALU.max),
                            r=pk, w=("1rt", pk))
                        P.step("vector", lambda E, ic=ic: E.tensor_tensor(
                            out=t1b[ic][:], in0=rtB[:], in1=rtB[:], op=ALU.mult),
                            r="1rt", w="1t1b")
                    for lc in range(4):
                        pb = ps[lc % 2]
                        pk = f"1rp{lc % 2}"
                        for c in range(4):
                            P.step("tensor", lambda E, c=c, lc=lc, e=e, pb=pb: E.matmul(
                                pb[:], w2[:, e, c, lc * 128:(lc + 1) * 128], t1b[c][:],
                                start=(c == 0), stop=(c == 3)),
                                r=("1w2e", "1t1b", pk), w=pk)
                        P.step("vector", lambda E, lc=lc, pb=pb: E.tensor_tensor(
                            out=rtB[:], in0=pb[:], in1=cb_sb[:], op=ALU.mult),
                            r=(pk, "1cbs"), w=("1rt", pk))
                        P.step("vector", lambda E, lc=lc: E.tensor_add(
                            comb[lc][:], comb[lc][:], rtB[:]), r=("1comb", "1rt"), w="1comb")
                combb = lat
                for lc in range(4):
                    P.step("vector", lambda E, lc=lc: E.tensor_copy(combb[lc][:], comb[lc][:]),
                           r="1comb", w="1lat")
                for oc in range(DC):
                    pb = ps[oc % 2]
                    pk = f"1rp{oc % 2}"
                    for c in range(4):
                        P.step("tensor", lambda E, c=c, oc=oc, pb=pb: E.matmul(
                            pb[:], fc2[:, c, oc * 128:(oc + 1) * 128], combb[c][:],
                            start=(c == 0), stop=(c == 3)),
                            r=("1f2w", "1lat", pk), w=pk)
                    P.step("scalar", lambda E, pb=pb: E.activation(out=rtB[:], in_=pb[:],
                           func=AF.Identity), r=pk, w=("1rt", pk))
                    P.dma("sync", io["f1"][oc, :, tsl], rtB[:], r="1rt", w="1f1d")
        with P.scope():
            up = P.sbuf("1up", [128, DC, 4096], BF16)
            for c in range(DC):
                P.dma("gpsimd", up[:, c, :], io["l1_up"][c * 128:(c + 1) * 128, :], w="1upw")
            down = P.sbuf("1dn", [128, 32, 1024], BF16)
            for c in range(32):
                P.dma("gpsimd", down[:, c, :], io["l1_down"][c * 128:(c + 1) * 128, :], w="1dnw")
            xnbC = [P.sbuf(f"1sx{c}", [128, 512], BF16) for c in range(DC)]
            hb = [P.sbuf(f"1hb{c}", [128, 512], BF16) for c in range(32)]
            ps = [P.psum(f"1sp{i}", [128, 512]) for i in range(2)]
            rtC = P.sbuf("1srt", [128, 512], F32)
            ft = P.sbuf("1sft", [128, 512], F32)
            for tb in range(T // 512):
                tsl = slice(tb * 512, (tb + 1) * 512)
                for c in range(DC):
                    P.dma("sync", xnbC[c][:], io["xn1"][c, :, tsl], w=f"1sx{c}")
                for hc in range(32):
                    pb = ps[hc % 2]
                    pk = f"1sp{hc % 2}"
                    for c in range(DC):
                        P.step("tensor", lambda E, c=c, hc=hc, pb=pb: E.matmul(
                            pb[:], up[:, c, hc * 128:(hc + 1) * 128], xnbC[c][:],
                            start=(c == 0), stop=(c == DC - 1)),
                            r=("1upw", f"1sx{c}", pk), w=pk)
                    P.step("vector", lambda E, pb=pb: E.tensor_scalar(
                        out=rtC[:], in0=pb[:], scalar1=0.0, scalar2=None, op0=ALU.max),
                        r=pk, w=("1srt", pk))
                    P.step("vector", lambda E, hc=hc: E.tensor_tensor(
                        out=hb[hc][:], in0=rtC[:], in1=rtC[:], op=ALU.mult), r="1srt", w="1hb")
                for oc in range(DC):
                    pb = ps[oc % 2]
                    pk = f"1sp{oc % 2}"
                    for c in range(32):
                        P.step("tensor", lambda E, c=c, oc=oc, pb=pb: E.matmul(
                            pb[:], down[:, c, oc * 128:(oc + 1) * 128], hb[c][:],
                            start=(c == 0), stop=(c == 31)),
                            r=("1dnw", "1hb", pk), w=pk)
                    P.dma("sync", ft[:], io["f1"][oc, :, tsl], r="1f1d", w="1sft")
                    P.step("vector", lambda E, pb=pb: E.tensor_tensor(
                        out=ft[:], in0=ft[:], in1=pb[:], op=ALU.add), r=("1sft", pk),
                        w=("1sft2", pk))
                    P.dma("sync", io["f1"][oc, :, tsl], ft[:], r="1sft2", w="1f1d")
    _l1_ffn()
    with P.scope():
        silu_combine_store(ctx, io["x1"], io["s1"], io["f1"], io["x2"])


def prep_layer3_consts(z):
    import ml_dtypes
    bf = ml_dtypes.bfloat16
    d = {}
    W = z["L3_attn_w_qkv"]
    cols = np.zeros((3072,), np.int64)
    for g in range(3):
        for h in range(16):
            for dd in range(64):
                cols[g * 1024 + h * 64 + dd] = h * 192 + g * 64 + dd
    d["l3_wqkv"] = np.ascontiguousarray(W[cols, :].T)
    d["l3_wo"] = np.ascontiguousarray(z["L3_attn_w_out"].T)
    d["l3_lnw"] = np.ascontiguousarray(z["L3_attn_ln_w"].reshape(8, 128).T)
    d["l3_lnb"] = np.ascontiguousarray(z["L3_attn_ln_b"].reshape(8, 128).T)
    mo = np.zeros((128, 128), np.float32)
    for a in range(128):
        for b in range(128):
            if a // 16 == b // 16:
                mo[a, b] = 1
    d["maskoct"] = mo.astype(bf)
    d["l3_w1t"] = np.ascontiguousarray(z["L3_ffn_W1"].T)
    d["l3_b1c"] = np.ascontiguousarray(z["L3_ffn_b1"].reshape(32, 128).T)
    d["l3_w2t"] = np.ascontiguousarray(z["L3_ffn_W2"].T)
    d["l3_b2c"] = np.ascontiguousarray(z["L3_ffn_b2"].reshape(8, 128).T)
    return d


def build_layer3(P, ctx, io):
    ln_phase(P, ctx, io["x2p"], io["xn3"], pfx="l3ln", xn32_dram=io["xn332"])
    l3g = P.sbuf("l3g", [128, 8], F32)
    P.dma("sync", l3g[:], io["l3_lnw"][:, :], w="lngb")
    l3b = P.sbuf("l3b", [128, 8], F32)
    P.dma("sync", l3b[:], io["l3_lnb"][:, :], w="lngb")
    ln_phase(P, ctx, io["xn332"], io["xin3"], pfx="l3ln2", gamma_cols=l3g, beta_cols=l3b)

    def _l3_attn(hf):
        TH = T // 2
        with P.scope():
            qf = P.sbuf("3qf", [128, DC, TH], BF16)
            kf = P.sbuf("3kf", [128, DC, TH], BF16)
            vtm = P.sbuf("3vtm", [128, 16 * 16 * 65], BF16)
            af = P.sbuf("3af", [128, DC, TH], BF16)
            negm = P.sbuf("3negm", [128, 16], F32)
            with P.scope():
                wq = P.sbuf("3wq", [128, DC, 3072], BF16)
                for c in range(DC):
                    P.dma("gpsimd", wq[:, c, :], io["l3_wqkv"][c * 128:(c + 1) * 128, :], w="3wq")
                xinb = [P.sbuf(f"3xi{c}", [128, 512], BF16) for c in range(DC)]
                ps = [P.psum(f"3ps{i}", [128, 512]) for i in range(2)]
                for tb in range(TH // 512):
                    tsl = slice(tb * 512, (tb + 1) * 512)
                    gsl = slice(hf * TH + tb * 512, hf * TH + (tb + 1) * 512)
                    for c in range(DC):
                        P.dma("sync", xinb[c][:], io["xin3"][c, :, gsl], w=f"3xi{c}")
                    for oc in range(16):
                        pb = ps[oc % 2]
                        pk = f"3p{oc % 2}"
                        for c in range(DC):
                            P.step("tensor", lambda E, c=c, oc=oc, pb=pb: E.matmul(
                                pb[:], wq[:, c, oc * 128:(oc + 1) * 128], xinb[c][:],
                                start=(c == 0), stop=(c == DC - 1)),
                                r=("3wq", f"3xi{c}", pk), w=pk)
                        dst = qf if oc < 8 else kf
                        P.step("scalar", lambda E, dst=dst, oc=oc, pb=pb, tsl=tsl: E.activation(
                            out=dst[:, oc % 8, tsl], in_=pb[:], func=AF.Identity),
                            r=pk, w=("3qk", pk))
                    for st_ in range(4):
                        tile = tb * 4 + st_
                        for half in range(2):
                            pb = ps[half]
                            pk = f"3p{half}"
                            for c in range(DC):
                                P.step("tensor", lambda E, c=c, st_=st_, half=half, pb=pb:
                                       E.matmul(pb[:], xinb[c][:, st_ * 128:(st_ + 1) * 128],
                                       wq[:, c, 2048 + half * 512:2048 + (half + 1) * 512],
                                       start=(c == 0), stop=(c == DC - 1)),
                                       r=("3wq", f"3xi{c}", pk), w=pk)
                            P.step("vector", lambda E, tile=tile, half=half, pb=pb: E.tensor_copy(
                                vtm[:, tile * 1040 + half * 520: tile * 1040 + (half + 1) * 520]
                                    .rearrange("p (h x) -> p h x", x=65)[:, :, 0:64],
                                pb[:].rearrange("p (h x) -> p h x", x=64)),
                                r=pk, w=("3vtm", pk))
                P.step("vector", lambda E: E.memset(
                    vtm[:].rearrange("p (t x) -> p t x", x=65)[:, :, 64:65], 1.0), w="3vtm")
                sq = P.sbuf("3sq", [128, 2048], BF16)
                ps_n2 = P.psum("3pn2", [16, 2048])
                selq = P.sbuf("3selq", [128, DC * 16], BF16)
                P.dma("sync", selq[:], io["selq"][:, :], w="3selq")
                ehrep = P.sbuf("3ehr", [16, 16 * 128], F32)
                P.dma("sync", ehrep[:], io["ehrep_f"][:, :], w="3ehr")
                n2r = P.sbuf("3n2r", [16, 2], F32)
                mcol = P.sbuf("3mc", [16, 1], F32)
                ps_c1 = P.psum("3pc1", [128, 1])
                for ci, src_ in ((0, qf), (1, kf)):
                    for c in range(DC):
                        P.step("scalar", lambda E, src_=src_, c=c: E.activation(
                            out=sq[:], in_=src_[:, c, :], func=AF.Square),
                            r=("3qk", "3sq2"), w="3sq")
                        for nb_ in range(4):
                            P.step("tensor", lambda E, c=c, nb_=nb_: E.matmul(
                                ps_n2[:, nb_ * 512:(nb_ + 1) * 512],
                                selq[:, c * 16:(c + 1) * 16], sq[:, nb_ * 512:(nb_ + 1) * 512],
                                start=(c == 0), stop=(c == DC - 1)),
                                r=("3sq", "3selq", "3n2"), w=("3n2", "3sq2"))
                    P.step("vector", lambda E, ci=ci: E.tensor_reduce(
                        out=n2r[:, ci:ci + 1], in_=ps_n2[:], axis=AX.X, op=ALU.max),
                        r="3n2", w=("3n2r", "3n2"))
                P.step("scalar", lambda E: E.activation(out=n2r[:], in_=n2r[:], func=AF.Sqrt),
                       r="3n2r", w="3n2r")
                P.step("vector", lambda E: E.tensor_tensor(out=mcol[:], in0=n2r[:, 0:1],
                       in1=n2r[:, 1:2], op=ALU.mult), r="3n2r", w="3mc0")
                P.step("vector", lambda E: E.tensor_scalar(out=mcol[:], in0=mcol[:],
                       scalar1=-1.0 / 32, scalar2=None, op0=ALU.mult), r="3mc0", w="3mc")
                for h in range(16):
                    P.step("tensor", lambda E, h=h: E.matmul(ps_c1[:], ehrep[:, h * 128:(h + 1) * 128],
                           mcol[:], start=True, stop=True), r=("3mc", "3ehr", "3nm"), w="3nm")
                    P.step("vector", lambda E, h=h: E.tensor_copy(negm[:, h:h + 1], ps_c1[:]),
                           r="3nm", w=("3negm", "3nm"))
            with P.scope():
                mko = P.sbuf("3mko", [128, 128], BF16)
                P.dma("sync", mko[:], io["maskoct"][:, :], w="3mko")
                W = P.sbuf("3W", [128, 128], BF16)
                ps_s = [P.psum(f"3ss{i}", [128, 128]) for i in range(2)]
                ps_o = P.psum("3po", [65, 128])
                ps_rb = P.psum("3prb", [64, 128])
                o_sb = P.sbuf("3osb", [65, 128], F32)
                rrow = P.sbuf("3rr", [1, 128], F32)
                onorm = P.sbuf("3on", [64, 128], BF16)
                for h in range(16):
                    c, half = h // 2, h % 2
                    rs = slice(half * 64, half * 64 + 64)
                    for o in range(16):
                        osl = slice(o * 128, (o + 1) * 128)
                        pb = ps_s[o % 2]
                        pk = f"3s{o % 2}"
                        P.step("tensor", lambda E, c=c, rs=rs, osl=osl, pb=pb: E.matmul(
                            pb[:], kf[rs, c, osl], qf[rs, c, osl], start=True, stop=True),
                            r=("3qk", pk), w=pk)
                        P.step("scalar", lambda E, h=h, pb=pb: E.activation(
                            out=W[:], in_=pb[:], func=AF.Exp, bias=negm[:, h:h + 1],
                            scale=1.0 / 32), r=(pk, "3negm", "3W"), w=("3W", pk))
                        P.step("vector", lambda E: E.tensor_mul(W[:], W[:], mko[:]),
                               r=("3W", "3mko"), w="3W")
                        P.step("tensor", lambda E, h=h, o=o: E.matmul(
                            ps_o[:], vtm[:, o * 1040 + h * 65: o * 1040 + (h + 1) * 65], W[:],
                            start=True, stop=True), r=("3W", "3vtm", "3po"), w="3po")
                        P.step("scalar", lambda E: E.activation(out=o_sb[:], in_=ps_o[:],
                               func=AF.Identity), r="3po", w=("3osb", "3po"))
                        P.step("vector", lambda E: E.reciprocal(rrow[:], o_sb[64:65, :]),
                               r="3osb", w="3rrk")
                        P.step("tensor", lambda E: E.matmul(ps_rb[:], ctx.onesrow_f[0:1, 0:64],
                               rrow[:], start=True, stop=True), r=("3rrk", "c2", "3prb"), w="3prb")
                        P.step("vector", lambda E, c=c, half=half, osl=osl: E.tensor_tensor(
                            out=af[half * 64:half * 64 + 64, c, osl], in0=o_sb[0:64, :],
                            in1=ps_rb[:], op=ALU.mult), r=("3osb", "3prb"), w=("3af", "3prb"))
            with P.scope():
                wo = P.sbuf("3wo", [128, DC, 1024], BF16)
                for c in range(DC):
                    P.dma("gpsimd", wo[:, c, :], io["l3_wo"][c * 128:(c + 1) * 128, :], w="3wo")
                xnb = [P.sbuf(f"3xn{c}", [128, 512], F32) for c in range(DC)]
                st = P.sbuf("3st", [128, 512], F32)
                ps = [P.psum(f"3wps{i}", [128, 512]) for i in range(2)]
                for tb in range(TH // 512):
                    tsl = slice(tb * 512, (tb + 1) * 512)
                    gsl = slice(hf * TH + tb * 512, hf * TH + (tb + 1) * 512)
                    for c in range(DC):
                        P.dma("sync", xnb[c][:], io["xn332"][c, :, gsl], w=f"3xn{c}")
                    for oc in range(DC):
                        pb = ps[oc % 2]
                        pk = f"3wp{oc % 2}"
                        for c in range(DC):
                            P.step("tensor", lambda E, c=c, oc=oc, pb=pb, tsl=tsl: E.matmul(
                                pb[:], wo[:, c, oc * 128:(oc + 1) * 128], af[:, c, tsl],
                                start=(c == 0), stop=(c == DC - 1)),
                                r=("3wo", "3af", pk), w=pk)
                        P.step("scalar", lambda E, pb=pb: E.activation(out=st[:], in_=pb[:],
                               func=AF.Identity), r=pk, w=("3st", pk))
                        P.step("vector", lambda E, oc=oc: E.tensor_add(st[:], st[:], xnb[oc][:]),
                               r=("3st", f"3xn{oc}"), w="3st")
                        P.dma("sync", io["s3"][oc, :, gsl], st[:], r="3st", w="3sd")
    for hf in range(2):
        _l3_attn(hf)

    def _l3_ffn():
        with P.scope():
            w1q = P.sbuf("3w1q", [128, DC, 4096], BF16)
            w2q = P.sbuf("3w2q", [128, 32, 1024], BF16)
            _qscope = P.scope(); _qscope.__enter__()
            wt = P.sbuf("3wt", [128, 4096], F32)
            absum = P.sbuf("3abs", [128, 1], F32)
            acc = P.sbuf("3acc", [128, 1], F32)
            gam = P.sbuf("3gam", [1, 1], F32)
            thc = P.sbuf("3thc", [128, 1], F32)
            ps_g = P.psum("3pg", [1, 1])
            ps_b = P.psum("3pb", [128, 1])
            sgn = P.sbuf("3sgn", [128, 4096], BF16)
            ind = P.sbuf("3ind", [128, 4096], F32)
            for wi, (nm, nchunk, ncol, wsb, tot) in enumerate(
                    (("l3_w1t", DC, 4096, w1q, 4096 * 1024),
                     ("l3_w2t", 32, 1024, w2q, 4096 * 1024))):
                for c in range(nchunk):
                    P.dma("sync", wt[:, 0:ncol], io[nm][c * 128:(c + 1) * 128, :], w="3wt")
                    P.step("vector", lambda E, c=c, ncol=ncol: E.tensor_reduce(
                        out=absum[:], in_=wt[:, 0:ncol], axis=AX.X, op=ALU.add,
                        apply_absolute_value=True), r="3wt", w=("3abs", "3wt"))
                    if c == 0:
                        P.step("vector", lambda E: E.tensor_copy(acc[:], absum[:]),
                               r="3abs", w="3acc")
                    else:
                        P.step("vector", lambda E: E.tensor_add(acc[:], acc[:], absum[:]),
                               r=("3acc", "3abs"), w="3acc")
                P.step("tensor", lambda E: E.matmul(ps_g[:], acc[:], ctx.ones128_f[0:128, 0:1],
                       start=True, stop=True), r=("3acc", "c0", "3psg"), w="3psg")
                P.step("vector", lambda E, tot=tot: E.tensor_scalar(out=gam[:], in0=ps_g[:],
                       scalar1=1.0 / tot, scalar2=1e-5, op0=ALU.mult, op1=ALU.add),
                       r="3psg", w=("3gam", "3psg"))
                P.step("vector", lambda E: E.tensor_scalar(out=gam[:], in0=gam[:],
                       scalar1=0.5, scalar2=None, op0=ALU.mult), r="3gam", w="3gam")
                P.step("tensor", lambda E: E.matmul(ps_b[:], ctx.onesrow_f[:], gam[:],
                       start=True, stop=True), r=("3gam", "c2", "3psb"), w="3psb")
                P.step("vector", lambda E: E.tensor_copy(thc[:], ps_b[:]), r="3psb",
                       w=("3thc", "3psb"))
                for c in range(nchunk):
                    P.dma("sync", wt[:, 0:ncol], io[nm][c * 128:(c + 1) * 128, :], w="3wt")
                    P.step("scalar", lambda E, ncol=ncol: E.activation(out=sgn[:, 0:ncol],
                           in_=wt[:, 0:ncol], func=AF.Sign), r="3wt", w="3sgn")
                    P.step("scalar", lambda E, ncol=ncol: E.activation(out=ind[:, 0:ncol],
                           in_=wt[:, 0:ncol], func=AF.Abs), r=("3wt", "3ind2"), w="3ind")
                    P.step("vector", lambda E, ncol=ncol: E.tensor_scalar(
                        out=ind[:, 0:ncol], in0=ind[:, 0:ncol], scalar1=thc[:, 0:1],
                        scalar2=None, op0=ALU.is_ge), r=("3ind", "3thc"), w="3ind")
                    P.step("vector", lambda E, c=c, ncol=ncol, wsb=wsb: E.tensor_tensor(
                        out=(wsb[:, c, :] if ncol == 4096 else wsb[:, c, :]),
                        in0=ind[:, 0:ncol], in1=sgn[:, 0:ncol], op=ALU.mult),
                        r=("3ind", "3sgn", "3wt"), w=("3wq", "3ind2"))
            if "w2qd" in io:
                for c in range(32):
                    P.dma("sync", io["w2qd"][c * 128:(c + 1) * 128, :], w2q[:, c, :], r="3wq", w="3wqd")
            if "w1qd" in io:
                for c in range(DC):
                    P.dma("sync", io["w1qd"][c * 128:(c + 1) * 128, :], w1q[:, c, :], r="3wq", w="3wqd")
            _qscope.__exit__(None, None, None)
            b1c = P.sbuf("3b1c", [128, 32], F32)
            P.dma("sync", b1c[:], io["l3_b1c"][:, :], w="3b1")
            b2c = P.sbuf("3b2c", [128, 8], F32)
            P.dma("sync", b2c[:], io["l3_b2c"][:, :], w="3b2")
            xnb = [P.sbuf(f"3fx{c}", [128, 512], BF16) for c in range(DC)]
            hb = [P.sbuf(f"3hb{c}", [128, 512], BF16) for c in range(32)]
            ps = [P.psum(f"3fp{i}", [128, 512]) for i in range(2)]
            ft = P.sbuf("3ft", [128, 512], F32)
            for tb in range(T // 512):
                tsl = slice(tb * 512, (tb + 1) * 512)
                for c in range(DC):
                    P.dma("sync", xnb[c][:], io["xn3"][c, :, tsl], w=f"3fx{c}")
                for hc in range(32):
                    pb = ps[hc % 2]
                    pk = f"3fp{hc % 2}"
                    for c in range(DC):
                        P.step("tensor", lambda E, c=c, hc=hc, pb=pb: E.matmul(
                            pb[:], w1q[:, c, hc * 128:(hc + 1) * 128], xnb[c][:],
                            start=(c == 0), stop=(c == DC - 1)),
                            r=("3wq", f"3fx{c}", pk), w=pk)
                    P.step("scalar", lambda E, hc=hc, pb=pb: E.activation(
                        out=hb[hc][:], in_=pb[:], func=AF.Relu, bias=b1c[:, hc:hc + 1]),
                        r=(pk, "3b1"), w=("3hb", pk))
                for oc in range(DC):
                    pb = ps[oc % 2]
                    pk = f"3fp{oc % 2}"
                    for c in range(32):
                        P.step("tensor", lambda E, c=c, oc=oc, pb=pb: E.matmul(
                            pb[:], w2q[:, c, oc * 128:(oc + 1) * 128], hb[c][:],
                            start=(c == 0), stop=(c == 31)),
                            r=("3wq", "3hb", pk), w=pk)
                    P.step("scalar", lambda E, oc=oc, pb=pb: E.activation(
                        out=ft[:], in_=pb[:], func=AF.Identity, bias=b2c[:, oc:oc + 1]),
                        r=(pk, "3b2"), w=("3ft", pk))
                    P.dma("sync", io["f3"][oc, :, tsl], ft[:], r="3ft", w="3fd")
    _l3_ffn()
    with P.scope():
        silu_combine_store(ctx, io["x2p"], io["s3"], io["f3"], io["x3"])


def prep_layer2_consts(z):
    import ml_dtypes
    import os
    import jax
    if 'cpu' not in (os.environ.get('JAX_PLATFORMS') or ''):
        try:
            jax.config.update('jax_platforms',
                              (os.environ.get('JAX_PLATFORMS') or 'cpu') + ',cpu'
                              if os.environ.get('JAX_PLATFORMS') else 'cpu')
        except Exception:
            pass
    import jax.numpy as jnp
    bf = ml_dtypes.bfloat16
    d = {}
    wqk, wv = z["L2_attn_w_qk"], z["L2_attn_w_v"]
    wil = np.zeros((1024, 2048), np.float32)
    for h in range(16):
        wil[:, h * 128:h * 128 + 64] = wqk[h * 64:(h + 1) * 64, :].T
        wil[:, h * 128 + 64:h * 128 + 128] = wv[h * 64:(h + 1) * 64, :].T
    d["l2_wil"] = wil
    with jax.default_device(jax.local_devices(backend='cpu')[0]):
        rot = np.asarray(jax.random.normal(jax.random.key(42), (16, 64, 4, 32),
                                           jnp.float32))
    d["l2_rot"] = np.ascontiguousarray(rot.reshape(16, 64, 128).transpose(1, 0, 2)
                                       .reshape(64, 16 * 128))
    d["iota32"] = np.tile(np.arange(32, dtype=np.float32)[None, None, :],
                          (128, 4, 1)).reshape(128, 128)
    d["iota64c"] = (np.arange(128, dtype=np.float32) % 64).reshape(128, 1)
    ps2 = np.zeros((128, 2), np.float32)
    ps2[0:64, 0] = 1; ps2[64:128, 1] = 1
    d["ps2"] = ps2
    hlf = np.zeros((128, 1), np.float32)
    hlf[64:128, 0] = 1
    d["hlfc"] = hlf
    lt2 = np.zeros((128, 128), np.float32)
    for p in range(128):
        for k in range(128):
            if (k // 64 == p // 64) and (k % 64) < (p % 64):
                lt2[k, p] = 1
    d["lt2"] = lt2
    selpr = np.zeros((4, 2 * 128), np.float32)
    for pr in range(2):
        selpr[2 * pr, pr * 128:pr * 128 + 64] = 1
        selpr[2 * pr + 1, pr * 128 + 64:pr * 128 + 128] = 1
    d["selpr"] = selpr
    d["l2_wo"] = np.ascontiguousarray(z["L2_attn_w_out"].T)
    d["l2_wg"] = z["L2_ffn_w_gate"]
    d["l2_wu"] = z["L2_ffn_w_up"]
    d["l2_wd"] = z["L2_ffn_w_down"]
    d["idf32"] = np.eye(128, dtype=np.float32)
    return d


def fp4_quant(P, pfx, xin, xq, n, rk, wk, tq, av, lv, sc, rsc):
    nb = n // 16
    k1, k2, k3, k4, k5 = pfx + "sc", pfx + "rs", pfx + "tq", pfx + "av", pfx + "lv"
    P.step("vector", lambda E: E.tensor_reduce(out=sc[:, 0:nb],
           in_=xin.rearrange("p (b k) -> p b k", k=16), axis=AX.X, op=ALU.max,
           apply_absolute_value=True), r=rk, w=k1)
    P.step("vector", lambda E: E.tensor_scalar(out=sc[:, 0:nb], in0=sc[:, 0:nb],
           scalar1=1.0 / 6.0, scalar2=1e-12, op0=ALU.mult, op1=ALU.max), r=k1, w=k1)
    P.step("vector", lambda E: E.reciprocal(rsc[:, 0:nb], sc[:, 0:nb]), r=k1, w=k2)
    P.step("vector", lambda E: E.tensor_tensor(
           out=tq[:, 0:n].rearrange("p (b k) -> p b k", k=16),
           in0=xin.rearrange("p (b k) -> p b k", k=16),
           in1=rsc[:, 0:nb].rearrange("p (b o) -> p b o", o=1).broadcast_to((128, nb, 16)),
           op=ALU.mult), r=(rk, k2), w=k3)
    P.step("scalar", lambda E: E.activation(out=av[:, 0:n], in_=tq[:, 0:n],
           func=AF.Abs), r=k3, w=k4)
    P.step("vector", lambda E: E.tensor_scalar(out=lv[:, 0:n], in0=av[:, 0:n],
           scalar1=0.25, scalar2=None, op0=ALU.is_gt), r=k4, w=k5)
    for th in (0.75, 1.25, 1.75):
        P.step("vector", lambda E, th=th: E.scalar_tensor_tensor(out=lv[:, 0:n],
               in0=av[:, 0:n], scalar=th, in1=lv[:, 0:n], op0=ALU.is_gt,
               op1=ALU.add), r=(k4, k5), w=k5)
    P.step("vector", lambda E: E.tensor_scalar(out=lv[:, 0:n], in0=lv[:, 0:n],
           scalar1=0.5, scalar2=None, op0=ALU.mult), r=k5, w=k5)
    for th in (2.5, 3.5):
        P.step("vector", lambda E, th=th: E.scalar_tensor_tensor(out=lv[:, 0:n],
               in0=av[:, 0:n], scalar=th, in1=lv[:, 0:n], op0=ALU.is_gt,
               op1=ALU.add), r=(k4, k5), w=k5)
    P.step("vector", lambda E: E.tensor_scalar(out=av[:, 0:n], in0=av[:, 0:n],
           scalar1=5.0, scalar2=2.0, op0=ALU.is_gt, op1=ALU.mult), r=k4, w=k4)
    P.step("vector", lambda E: E.tensor_add(lv[:, 0:n], lv[:, 0:n], av[:, 0:n]),
           r=(k4, k5), w=k5)
    P.step("scalar", lambda E: E.activation(out=tq[:, 0:n], in_=tq[:, 0:n],
           func=AF.Sign), r=k3, w=k3)
    P.step("vector", lambda E: E.tensor_mul(lv[:, 0:n], lv[:, 0:n], tq[:, 0:n]),
           r=(k3, k5), w=k5)
    P.step("vector", lambda E: E.tensor_tensor(
           out=xq.rearrange("p (b k) -> p b k", k=16),
           in0=lv[:, 0:n].rearrange("p (b k) -> p b k", k=16),
           in1=sc[:, 0:nb].rearrange("p (b o) -> p b o", o=1).broadcast_to((128, nb, 16)),
           op=ALU.mult), r=(k1, k5), w=wk)


def build_layer2(P, ctx, io, nheads=16,
                 phases=("ln", "proj", "attn", "chunk", "oproj", "ffn", "cmb"),
                 bh_limit=64):
    if "ln" in phases:
        ln_phase(P, ctx, io["x2i"], io["xn2"], pfx="l2ln", xn32_dram=io["xn232"])

    def _l2_proj():
        with P.scope():
            wil = P.sbuf("2wil", [128, DC, 2048], BF16)
            for c in range(DC):
                P.dma("gpsimd", wil[:, c, :], io["l2_wil"][c * 128:(c + 1) * 128, :], w="2wil")
            xnb = [P.sbuf(f"2px{c}", [128, 512], BF16) for c in range(DC)]
            ps = [P.psum(f"2pp{i}", [128, 512]) for i in range(2)]
            ot = P.sbuf("2pot", [128, 512], BF16)
            for tb in range(T // 512):
                tsl = slice(tb * 512, (tb + 1) * 512)
                for c in range(DC):
                    P.dma("sync", xnb[c][:], io["xn2"][c, :, tsl], w=f"2px{c}")
                for oc in range(16):
                    pb = ps[oc % 2]
                    pk = f"2pp{oc % 2}"
                    for c in range(DC):
                        P.step("tensor", lambda E, c=c, oc=oc, pb=pb: E.matmul(
                            pb[:], wil[:, c, oc * 128:(oc + 1) * 128], xnb[c][:],
                            start=(c == 0), stop=(c == DC - 1)),
                            r=("2wil", f"2px{c}", pk), w=pk)
                    P.step("vector", lambda E, pb=pb: E.tensor_copy(ot[:], pb[:]),
                           r=pk, w=("2pot", pk))
                    P.dma("sync", io["qkvil"][oc, :, tsl], ot[:], r="2pot", w="2qild")
    if "proj" in phases:
        _l2_proj()

    def _l2_attn(b, h):
        bh = b * 16 + h
        with P.scope():
            qkv = P.sbuf("2qkv", [128, S], BF16)
            P.dma("sync", qkv[:], io["qkvil"][h, :, b * S:(b + 1) * S], r="2qild", w="2qkv")
            rot = P.sbuf("2rot", [64, 128], BF16)
            P.dma("gpsimd", rot[:], io["l2_rot"][:, h * 128:(h + 1) * 128], w="2rot")
            iota32 = P.sbuf("2i32", [128, 128], F32)
            P.dma("sync", iota32[:], io["iota32"][:, :], w="2i32")
            iota64 = P.sbuf("2i64", [128, 1], F32)
            P.dma("sync", iota64[:], io["iota64c"][:, :], w="2i64")
            ps2 = P.sbuf("2ps2", [128, 2], F32)
            P.dma("sync", ps2[:], io["ps2"][:, :], w="2ps2c")
            hlfc = P.sbuf("2hlf", [128, 1], F32)
            P.dma("sync", hlfc[:], io["hlfc"][:, :], w="2hlfc")
            lt2 = P.sbuf("2lt2", [128, 128], F32)
            P.dma("sync", lt2[:], io["lt2"][:, :], w="2lt2")
            selpr = P.sbuf("2selp", [4, 256], F32)
            P.dma("sync", selpr[:], io["selpr"][:, :], w="2selp")
            idf = P.sbuf("2idf", [128, 128], F32)
            P.dma("sync", idf[:], io["idf32"][:, :], w="2idf")
            idb = P.sbuf("2idb", [128, 128], BF16)
            P.dma("sync", idb[:], io["id128b"][:, :], w="2idb")
            ps_rv = P.psum("2prv", [128, 128])
            ps_t = P.psum("2pt", [128, 128])
            rv = P.sbuf("2rv", [128, 128], F32)
            ma = P.sbuf("2ma", [128, 4], F32)
            mn = P.sbuf("2mn", [128, 4], F32)
            w1 = P.sbuf("2w1", [128, 128], F32)
            w2 = P.sbuf("2w2", [128, 128], F32)
            bkt = P.sbuf("2bk", [128, 4], F32)
            brow = P.sbuf("2br", [4, S], F32)
            for tile in range(16):
                P.step("tensor", lambda E, tile=tile: E.matmul(ps_rv[:],
                       qkv[0:64, tile * 128:(tile + 1) * 128], rot[:],
                       start=True, stop=True), r=("2qkv", "2rot", "2prv"), w="2prv")
                P.step("scalar", lambda E: E.activation(out=rv[:], in_=ps_rv[:],
                       func=AF.Identity), r="2prv", w=("2rv", "2prv"))
                P.step("vector", lambda E: E.tensor_reduce(out=ma[:],
                       in_=rv[:].rearrange("p (r k) -> p r k", k=32), axis=AX.X,
                       op=ALU.max), r="2rv", w="2ma")
                P.step("vector", lambda E: E.tensor_reduce(out=mn[:],
                       in_=rv[:].rearrange("p (r k) -> p r k", k=32), axis=AX.X,
                       op=ALU.min), r="2rv", w="2mn")
                P.step("vector", lambda E: E.tensor_tensor(out=w1[:].rearrange(
                       "p (r k) -> p r k", k=32),
                       in0=rv[:].rearrange("p (r k) -> p r k", k=32),
                       in1=ma[:].rearrange("p (r o) -> p r o", o=1).broadcast_to((128, 4, 32)),
                       op=ALU.is_equal), r=("2rv", "2ma"), w="2w1")
                P.step("vector", lambda E: E.tensor_tensor(out=w2[:].rearrange(
                       "p (r k) -> p r k", k=32),
                       in0=rv[:].rearrange("p (r k) -> p r k", k=32),
                       in1=mn[:].rearrange("p (r o) -> p r o", o=1).broadcast_to((128, 4, 32)),
                       op=ALU.is_equal), r=("2rv", "2mn"), w="2w2")
                P.step("vector", lambda E: E.tensor_add(mn[:], mn[:], ma[:]),
                       r=("2mn", "2ma"), w=("2mn2", "2mn"))
                P.step("vector", lambda E: E.tensor_scalar(out=mn[:], in0=mn[:],
                       scalar1=0.0, scalar2=None, op0=ALU.is_ge), r="2mn2", w="2sel")
                P.step("vector", lambda E: E.tensor_tensor(out=w1[:].rearrange(
                       "p (r k) -> p r k", k=32),
                       in0=w1[:].rearrange("p (r k) -> p r k", k=32),
                       in1=mn[:].rearrange("p (r o) -> p r o", o=1).broadcast_to((128, 4, 32)),
                       op=ALU.mult), r=("2w1", "2sel"), w="2w1")
                P.step("vector", lambda E: E.tensor_scalar(out=mn[:], in0=mn[:],
                       scalar1=-1.0, scalar2=1.0, op0=ALU.mult, op1=ALU.add),
                       r="2sel", w=("2seln", "2sel"))
                P.step("vector", lambda E: E.tensor_tensor(out=w2[:].rearrange(
                       "p (r k) -> p r k", k=32),
                       in0=w2[:].rearrange("p (r k) -> p r k", k=32),
                       in1=mn[:].rearrange("p (r o) -> p r o", o=1).broadcast_to((128, 4, 32)),
                       op=ALU.mult), r=("2w2", "2seln"), w="2w2")
                P.step("vector", lambda E: E.tensor_add(w1[:], w1[:], w2[:]),
                       r=("2w1", "2w2"), w="2w1")
                P.step("vector", lambda E: E.tensor_mul(w1[:], w1[:], iota32[:]),
                       r=("2w1", "2i32"), w="2w1f")
                P.step("vector", lambda E: E.tensor_reduce(out=bkt[:],
                       in_=w1[:].rearrange("p (r k) -> p r k", k=32), axis=AX.X,
                       op=ALU.add), r="2w1f", w="2bk0")
                P.step("vector", lambda E: E.tensor_scalar(out=mn[:], in0=mn[:],
                       scalar1=32.0, scalar2=None, op0=ALU.mult), r="2seln",
                       w=("2m32", "2seln"))
                P.step("vector", lambda E: E.tensor_add(bkt[:], bkt[:], mn[:]),
                       r=("2bk0", "2m32"), w="2bk")
                P.step("tensor", lambda E: E.matmul(ps_t[0:4, :], bkt[:], idf[:],
                       start=True, stop=True), r=("2bk", "2idf", "2ptt"), w="2ptt")
                P.step("vector", lambda E, tile=tile: E.tensor_copy(
                    brow[:, tile * 128:(tile + 1) * 128], ps_t[0:4, :]),
                    r="2ptt", w=("2brow", "2ptt"))
            if "browd" in io:
                P.dma("sync", io["browd"][bh, :, :], brow[0:4, :], r="2brow",
                      w="2browd")
            Opr = P.sbuf("2op", [128, S], BF16)
            incl = P.sbuf("2incl", [128, S], F32)
            cnt = P.sbuf("2cnt", [128, 1], F32)
            radd = P.sbuf("2radd", [128, 1], F32)
            prod = P.sbuf("2prod", [128, S], F32)
            posrow = P.sbuf("2posr", [2, S], F32)
            ps_bb = P.psum("2pbb", [128, 512])
            ps_off = P.psum("2poff", [128, 1])
            ps_pos = P.psum("2ppos", [2, 512])
            ps_t2 = P.psum("2pt2", [128, 2])
            pos_tm = P.sbuf("2postm", [128, 16 * 4], F32)
            posu = P.sbuf("2posu", [128, 1], U32)
            for pr in range(2):
                for nb_ in range(4):
                    P.step("tensor", lambda E, pr=pr, nb_=nb_: E.matmul(
                        ps_bb[:], selpr[:, pr * 128:(pr + 1) * 128],
                        brow[0:4, nb_ * 512:(nb_ + 1) * 512],
                        start=True, stop=True), r=("2brow", "2selp", "2pbb"), w="2pbb")
                    P.step("vector", lambda E, nb_=nb_: E.tensor_scalar(
                        out=Opr[:, nb_ * 512:(nb_ + 1) * 512], in0=ps_bb[:],
                        scalar1=iota64[:, 0:1], scalar2=None, op0=ALU.is_equal),
                        r=("2pbb", "2i64"), w=("2Opr", "2pbb"))
                P.step("vector", lambda E: E.tensor_tensor_scan(out=incl[:],
                       data0=ctx.ones128_f[:].broadcast_to((128, S)), data1=Opr[:],
                       initial=0.0, op0=ALU.mult, op1=ALU.add),
                       r=("2Opr", "c0"), w="2incl")
                P.step("vector", lambda E: E.tensor_copy(cnt[:], incl[:, S - 1:S]),
                       r="2incl", w="2cnt")
                P.step("tensor", lambda E: E.matmul(ps_off[:], lt2[:], cnt[:],
                       start=True, stop=True), r=("2cnt", "2lt2", "2poff"), w="2poff")
                P.step("vector", lambda E, pr=pr: E.tensor_scalar(out=radd[:],
                       in0=hlfc[:], scalar1=float(S), scalar2=float((2 * pr) * S - 1),
                       op0=ALU.mult, op1=ALU.add), r="2hlfc", w="2radd")
                P.step("vector", lambda E: E.tensor_tensor(out=radd[:], in0=radd[:],
                       in1=ps_off[:], op=ALU.add), r=("2radd", "2poff"),
                       w=("2radd", "2poff"))
                P.step("vector", lambda E: E.scalar_tensor_tensor(out=prod[:],
                       in0=incl[:], scalar=radd[:, 0:1], in1=Opr[:], op0=ALU.add,
                       op1=ALU.mult), r=("2incl", "2radd", "2Opr"), w="2prod")
                for nb_ in range(4):
                    P.step("tensor", lambda E, nb_=nb_: E.matmul(
                        ps_pos[:], ps2[:, 0:2], prod[:, nb_ * 512:(nb_ + 1) * 512],
                        start=True, stop=True), r=("2prod", "2ps2c", "2ppos"), w="2ppos")
                    P.step("scalar", lambda E, nb_=nb_: E.activation(
                        out=posrow[:, nb_ * 512:(nb_ + 1) * 512], in_=ps_pos[:],
                        func=AF.Identity), r="2ppos", w=("2posr", "2ppos"))
                for tile in range(16):
                    P.step("tensor", lambda E, tile=tile: E.matmul(ps_t2[:],
                           posrow[0:2, tile * 128:(tile + 1) * 128], idf[0:2, 0:2],
                           start=True, stop=True), r=("2posr", "2idf", "2pt2"), w="2pt2")
                    P.step("vector", lambda E, tile=tile, pr=pr: E.tensor_copy(
                        pos_tm[:, tile * 4 + pr * 2: tile * 4 + pr * 2 + 2],
                        ps_t2[0:128, 0:2]), r="2pt2", w=("2postm", "2pt2"))
            P.dma("sync", io["posd"][bh, :, :], pos_tm[:], r="2postm", w="2posd")
            qkvtm = P.sbuf("2qtm", [128, 128], BF16)
            ps_tr = P.psum("2ptr", [128, 128], BF16)
            for tile in range(16):
                P.step("tensor", lambda E, tile=tile: E.transpose(ps_tr[:],
                       qkv[:, tile * 128:(tile + 1) * 128], idb[:]),
                       r=("2qkv", "2idb", "2ptr"), w="2ptr")
                P.step("vector", lambda E: E.tensor_copy(qkvtm[:], ps_tr[:]),
                       r="2ptr", w=("2qtm", "2ptr"))
                for r_ in range(4):
                    P.step("vector", lambda E, tile=tile, r_=r_: E.tensor_copy(
                        posu[:], pos_tm[:, tile * 4 + r_: tile * 4 + r_ + 1]),
                        r=("2postm", "2posu2"), w="2posu")
                    def fn(E, bh=bh):
                        return E.indirect_dma_start(out=io["qkvs"][:, :],
                                                    out_offset=bass.IndirectOffsetOnAxis(
                                                        ap=posu[:, 0:1], axis=0),
                                                    in_=qkvtm[:], in_offset=None,
                                                    element_offset=bh * 8192 * 128)
                    P.step("gpsimd", fn, r=("2qtm", "2posu", "2sc"), w=("2sc", "2posu2"),
                           dma=True)

    if "attn" in phases:
        for b in range(NB):
            for h in range(nheads):
                if b * 16 + h < bh_limit:
                    _l2_attn(b, h)

    def _l2_chunk(b, h):
        bh = b * 16 + h
        qs_base = bh * 8192
        with P.scope():
            qk_fm = P.sbuf("2qf", [128, 8192], BF16)
            vtm = P.sbuf("2vt", [128, 64 * 65], BF16)
            vtm2 = P.sbuf("2vt2", [128, 63 * 65], BF16)
            kn = P.sbuf("2kn", [64, 8192], BF16)
            osb = P.sbuf("2osb", [65, 8192], F32)
            msr = P.sbuf("2msr", [1, 8192], F32)
            sqq = P.sbuf("2sqq", [64, 2048], F32)
            Wt = P.sbuf("2Wt", [128, 64], BF16)
            negm = P.sbuf("2ngm", [128, 1], F32)
            mx = P.sbuf("2mx", [1, 1], F32)
            idf2 = P.sbuf("2idf2", [128, 128], F32)
            P.dma("sync", idf2[:], io["idf32"][:, :], w="2idf2")
            idb2 = P.sbuf("2idb2", [128, 128], BF16)
            P.dma("sync", idb2[:], io["id128b"][:, :], w="2idb2")
            postm2 = P.sbuf("2ptm2", [128, 64], F32)
            P.dma("sync", postm2[:], io["posd"][bh, :, :], r="2posd", w="2postma")
            ps_ms = P.psum("2pms", [1, 512])
            ps_rb = P.psum("2prb", [64, 512])
            ps_b1 = P.psum("2pb1", [128, 1])
            ps_s = [P.psum("2cs0", [128, 64])] * 2
            ps_o = P.psum("2po2", [65, 64])
            P.step("sync", lambda E, qs_base=qs_base: E.dma_start(
                out=qk_fm[:], in_=io["qkvs"][qs_base:qs_base + 8192, :],
                transpose=True), r="2sc", w="2qf", dma=True)
            P.dma("sync", vtm[:].rearrange("p (t x) -> p t x", x=65)[:, :, 0:64],
                  io["qkvs"][qs_base:qs_base + 8192, 64:128]
                  .rearrange("(t p) d -> p t d", p=128), r="2sc", w="2vt")
            P.dma("sync", vtm2[:].rearrange("p (t x) -> p t x", x=65)[:, :, 0:64],
                  io["qkvs"][qs_base + 64:qs_base + 8192 - 64, 64:128]
                  .rearrange("(t p) d -> p t d", p=128), r="2sc", w="2vt2")
            P.step("vector", lambda E: E.memset(
                vtm[:].rearrange("p (t x) -> p t x", x=65)[:, :, 64:65], 1.0), w="2vt")
            P.step("vector", lambda E: E.memset(
                vtm2[:].rearrange("p (t x) -> p t x", x=65)[:, :, 64:65], 1.0), w="2vt2")
            for half in range(4):
                hsl = slice(half * 2048, (half + 1) * 2048)
                P.step("scalar", lambda E, hsl=hsl: E.activation(out=sqq[:],
                       in_=qk_fm[0:64, hsl], func=AF.Square), r=("2qf", "2sq2"), w="2sq")
                for nb_ in range(4):
                    P.step("tensor", lambda E, half=half, nb_=nb_: E.matmul(
                        ps_ms[:], ctx.ones128_f[0:64, 0:1],
                        sqq[:, nb_ * 512:(nb_ + 1) * 512], start=True, stop=True),
                        r=("2sq", "c0", "2pms"), w="2pms")
                    P.step("vector", lambda E, half=half, nb_=nb_: E.tensor_scalar(
                        out=msr[:, half * 2048 + nb_ * 512: half * 2048 + (nb_ + 1) * 512],
                        in0=ps_ms[:], scalar1=1.0 / 64, scalar2=1e-6, op0=ALU.mult,
                        op1=ALU.add), r="2pms", w=("2msr", "2pms", "2sq2"))
            P.step("scalar", lambda E: E.activation(out=msr[:], in_=msr[:], func=AF.Sqrt),
                   r="2msr", w="2msr")
            P.step("vector", lambda E: E.tensor_scalar(out=msr[:], in0=msr[:], scalar1=8.0,
                   scalar2=None, op0=ALU.mult), r="2msr", w="2msr")
            P.step("vector", lambda E: E.reciprocal(msr[:], msr[:]), r="2msr", w="2msr")
            for nb_ in range(16):
                P.step("tensor", lambda E, nb_=nb_: E.matmul(ps_rb[:],
                       ctx.onesrow_f[0:1, 0:64], msr[:, nb_ * 512:(nb_ + 1) * 512],
                       start=True, stop=True), r=("2msr", "c2", "2prb"), w="2prb")
                P.step("vector", lambda E, nb_=nb_: E.tensor_tensor(
                    out=kn[:, nb_ * 512:(nb_ + 1) * 512],
                    in0=qk_fm[0:64, nb_ * 512:(nb_ + 1) * 512], in1=ps_rb[:], op=ALU.mult),
                    r=("2qf", "2prb"), w=("2kn", "2prb"))
            P.step("vector", lambda E: E.tensor_reduce(out=mx[:], in_=msr[:], axis=AX.X,
                   op=ALU.min), r="2msr", w="2mx")
            P.step("vector", lambda E: E.reciprocal(mx[:], mx[:]), r="2mx", w="2mx")
            P.step("vector", lambda E: E.tensor_scalar(out=mx[:], in0=mx[:], scalar1=-1.0,
                   scalar2=None, op0=ALU.mult), r="2mx", w="2mx")
            P.step("tensor", lambda E: E.matmul(ps_b1[:], ctx.onesrow_f[:], mx[:],
                   start=True, stop=True), r=("2mx", "c2", "2pb1"), w="2pb1")
            P.step("vector", lambda E: E.tensor_copy(negm[:], ps_b1[:]), r="2pb1",
                   w=("2ngm", "2pb1"))
            vwrap = P.sbuf("2vw", [64, 65], BF16)
            P.step("vector", lambda E: E.tensor_copy(vwrap[:],
                   vtm[64:128, 63 * 65:64 * 65]), r="2vt", w="2vw")
            WA = P.sbuf("2WA", [64, 64], BF16)
            ps_sa = P.psum("2csa", [64, 64])
            for cg in range(128):
                pb = ps_s[0]
                pk = "2cs0"
                if cg == 0:
                    P.step("tensor", lambda E: E.matmul(ps_sa[:], kn[:, 8128:8192],
                           qk_fm[0:64, 0:64], start=True, stop=True),
                           r=("2kn", "2qf", "2csa"), w="2csa")
                    P.step("scalar", lambda E: E.activation(out=WA[:], in_=ps_sa[:],
                           func=AF.Exp, bias=negm[0:64, 0:1]),
                           r=("2csa", "2ngm", "2WA"), w=("2WA", "2csa"))
                    P.step("tensor", lambda E, pb=pb: E.matmul(pb[0:64, :], kn[:, 0:64],
                           qk_fm[0:64, 0:64], start=True, stop=True),
                           r=("2kn", "2qf", pk), w=pk)
                    P.step("scalar", lambda E, pb=pb: E.activation(out=Wt[0:64, :],
                           in_=pb[0:64, :], func=AF.Exp, bias=negm[0:64, 0:1]),
                           r=(pk, "2ngm", "2Wt"), w=("2Wt", pk))
                    P.step("tensor", lambda E: E.matmul(ps_o[:], vwrap[:], WA[:],
                           start=True, stop=False), r=("2WA", "2vw", "2po2"), w="2po2")
                    P.step("tensor", lambda E: E.matmul(ps_o[:], vtm[0:64, 0:65],
                           Wt[0:64, :], start=False, stop=True),
                           r=("2Wt", "2vt", "2po2"), w="2po2")
                else:
                    w0 = cg * 64 - 64
                    P.step("tensor", lambda E, w0=w0, cg=cg, pb=pb: E.matmul(
                           pb[:], kn[:, w0:w0 + 128], qk_fm[0:64, cg * 64:(cg + 1) * 64],
                           start=True, stop=True), r=("2kn", "2qf", pk), w=pk)
                    P.step("scalar", lambda E, pb=pb: E.activation(out=Wt[:], in_=pb[:],
                           func=AF.Exp, bias=negm[:, 0:1]), r=(pk, "2ngm", "2Wt"),
                           w=("2Wt", pk))
                    if cg % 2 == 1:
                        vt_, tl, vk = vtm, (cg - 1) // 2, "2vt"
                    else:
                        vt_, tl, vk = vtm2, (cg - 2) // 2, "2vt2"
                    P.step("tensor", lambda E, vt_=vt_, tl=tl: E.matmul(ps_o[:],
                           vt_[:, tl * 65:(tl + 1) * 65], Wt[:], start=True, stop=True),
                           r=("2Wt", vk, "2po2"), w="2po2")
                P.step("scalar", lambda E, cg=cg: E.activation(
                    out=osb[:, cg * 64:(cg + 1) * 64], in_=ps_o[:], func=AF.Identity),
                    r="2po2", w=("2osb", "2po2"))
            ps_to = P.psum("2pto", [128, 65])
            otr = P.sbuf("2otr", [128, 65], F32)
            for tile in range(64):
                P.step("tensor", lambda E, tile=tile: E.transpose(ps_to[:],
                       osb[:, tile * 128:(tile + 1) * 128], idf2[0:65, 0:65]),
                       r=("2osb", "2idf2", "2pto"), w="2pto")
                P.step("vector", lambda E: E.tensor_copy(otr[:], ps_to[:]),
                       r="2pto", w=("2otr", "2pto"))
                P.dma("sync", io["otm"][tile * 128:(tile + 1) * 128, :], otr[:],
                      r="2otr", w="2otmd")
            gat = [P.sbuf(f"2ga{r_}", [128, 65], F32) for r_ in range(4)]
            posu2 = P.sbuf("2pu2", [128, 1], U32)
            rec = P.sbuf("2rec", [128, 1], F32)
            att = P.sbuf("2att", [128, 64], BF16)
            ps_ta = P.psum("2pta", [64, 128], BF16)
            attr_ = P.sbuf("2attr", [64, 128], BF16)
            for tile in range(16):
                for r_ in range(4):
                    P.step("vector", lambda E, tile=tile, r_=r_: E.tensor_copy(
                        posu2[:], postm2[:, tile * 4 + r_: tile * 4 + r_ + 1]),
                        r=("2postma", "2pu22"), w="2pu2")
                    def fng(E, r_=r_):
                        return E.indirect_dma_start(out=gat[r_][:], out_offset=None,
                                                    in_=io["otm"][:, :],
                                                    in_offset=bass.IndirectOffsetOnAxis(
                                                        ap=posu2[:, 0:1], axis=0))
                    P.step("gpsimd", fng, r=("2otmd", "2pu2"), w=f"2ga{r_}", dma=True)
                P.step("vector", lambda E: E.tensor_add(gat[0][:], gat[0][:], gat[1][:]),
                       r=("2ga0", "2ga1"), w="2ga0")
                P.step("vector", lambda E: E.tensor_add(gat[2][:], gat[2][:], gat[3][:]),
                       r=("2ga2", "2ga3"), w="2ga2")
                P.step("vector", lambda E: E.tensor_add(gat[0][:], gat[0][:], gat[2][:]),
                       r=("2ga0", "2ga2"), w="2ga0")
                P.step("vector", lambda E: E.reciprocal(rec[:], gat[0][:, 64:65]),
                       r="2ga0", w="2rec")
                P.step("vector", lambda E: E.tensor_scalar(out=att[:], in0=gat[0][:, 0:64],
                       scalar1=rec[:, 0:1], scalar2=None, op0=ALU.mult),
                       r=("2ga0", "2rec"), w="2attk")
                P.step("tensor", lambda E: E.transpose(ps_ta[:], att[:], idb2[:]),
                       r=("2attk", "2idb2", "2pta"), w="2pta")
                P.step("vector", lambda E: E.tensor_copy(attr_[:], ps_ta[0:64, :]),
                       r="2pta", w=("2attr", "2pta", "2pu22"))
                P.dma("sync", io["attnil"][h, :, b * S + tile * 128: b * S + (tile + 1) * 128],
                      attr_[:], r="2attr", w="2attd")

    if "chunk" in phases:
        for b in range(NB):
            for h in range(nheads):
                if b * 16 + h < bh_limit:
                    _l2_chunk(b, h)

    def _l2_oproj():
        with P.scope():
            wo = P.sbuf("2wo", [128, DC, 1024], BF16)
            for c in range(DC):
                P.dma("gpsimd", wo[:, c, :], io["l2_wo"][c * 128:(c + 1) * 128, :],
                      w="2wo")
            ab = [P.sbuf(f"2ab{c}", [128, 512], BF16) for c in range(DC)]
            ps = [P.psum(f"2so{i}", [128, 512]) for i in range(2)]
            ot = P.sbuf("2sot", [128, 512], F32)
            for tb in range(T // 512):
                tsl = slice(tb * 512, (tb + 1) * 512)
                for c in range(DC):
                    P.dma("sync", ab[c][:],
                          io["attnil"][2 * c:2 * c + 2, :, tsl]
                          .rearrange("a p t -> (a p) t"), r="2attd", w=f"2ab{c}")
                for oc in range(DC):
                    pb = ps[oc % 2]
                    pk = f"2so{oc % 2}"
                    for c in range(DC):
                        P.step("tensor", lambda E, c=c, oc=oc, pb=pb: E.matmul(
                            pb[:], wo[:, c, oc * 128:(oc + 1) * 128], ab[c][:],
                            start=(c == 0), stop=(c == DC - 1)),
                            r=("2wo", f"2ab{c}", pk), w=pk)
                    P.step("vector", lambda E, pb=pb: E.tensor_copy(ot[:], pb[:]),
                           r=pk, w=("2sot", pk))
                    P.dma("sync", io["s2"][oc, :, tsl], ot[:], r="2sot", w="2s2d")
    if "oproj" in phases:
        _l2_oproj()

    NI = 2816 // 128

    def _l2_ffn():
        with P.scope():
            wgq = P.sbuf("2wgq", [128, DC, 2816], BF16)
            wuq = P.sbuf("2wuq", [128, DC, 2816], BF16)
            wdq = P.sbuf("2wdq", [128, NI, 1024], BF16)
            idb3 = P.sbuf("2idb3", [128, 128], BF16)
            P.dma("sync", idb3[:], io["id128b"][:, :], w="2idb3")
            idf3 = P.sbuf("2idf3", [128, 128], F32)
            P.dma("sync", idf3[:], io["idf32"][:, :], w="2idf3")
            with P.scope():
                wt = P.sbuf("2qwt", [128, 2816], F32)
                tq = P.sbuf("2qtq", [128, 2816], F32)
                av = P.sbuf("2qav", [128, 2816], F32)
                lv = P.sbuf("2qlv", [128, 2816], F32)
                sc = P.sbuf("2qsc", [128, 176], F32)
                rsc = P.sbuf("2qrs", [128, 176], F32)
                wqb = P.sbuf("2qwb", [128, 2816], BF16)
                ps_tw = P.psum("2ptw", [128, 128], BF16)
                for (wname, dst, dk, nt, kn) in (
                        ("l2_wg", wgq, "2wgq", NI, 1024),
                        ("l2_wu", wuq, "2wuq", NI, 1024),
                        ("l2_wd", wdq, "2wdq", DC, 2816)):
                    for t_ in range(nt):
                        P.dma("sync", wt[:, 0:kn],
                              io[wname][t_ * 128:(t_ + 1) * 128, :], w="2qwt")
                        fp4_quant(P, "2qw", wt[:, 0:kn], wqb[:, 0:kn], kn,
                                  "2qwt", "2qwb", tq, av, lv, sc, rsc)
                        for c in range(kn // 128):
                            P.step("tensor", lambda E, c=c: E.transpose(ps_tw[:],
                                   wqb[:, c * 128:(c + 1) * 128], idb3[:]),
                                   r=("2qwb", "2idb3", "2ptw"), w="2ptw")
                            P.step("vector", lambda E, c=c, t_=t_, dst=dst, dk=dk:
                                   E.tensor_copy(dst[:, c, t_ * 128:(t_ + 1) * 128],
                                                 ps_tw[:]), r="2ptw", w=(dk, "2ptw"))
            with P.scope():
                xfm = P.sbuf("2xfm", [128, DC, 128], F32)
                xt = P.sbuf("2xt", [128, 1024], F32)
                xqt = P.sbuf("2xqt", [128, 1024], BF16)
                xqf = P.sbuf("2xqf", [128, DC, 128], BF16)
                htm = P.sbuf("2htm", [128, 2816], F32)
                hqt = P.sbuf("2hqt", [128, 2816], BF16)
                hqf = P.sbuf("2hqf", [128, NI, 128], BF16)
                gt_ = P.sbuf("2gt", [128, 512], F32)
                fot = P.sbuf("2fot", [128, 128], F32)
                tq = P.sbuf("2ftq", [128, 2816], F32)
                av = P.sbuf("2fav", [128, 2816], F32)
                lv = P.sbuf("2flv", [128, 2816], F32)
                sc = P.sbuf("2fsc", [128, 176], F32)
                rsc = P.sbuf("2frs", [128, 176], F32)
                ps_tx = P.psum("2ptx", [128, 128])
                ps_tb = P.psum("2ptb", [128, 128], BF16)
                ps_g = P.psum("2pg", [128, 512])
                ps_u = P.psum("2pu3", [128, 512])
                ps_d = P.psum("2pd", [128, 128])
                for tt in range(T // 128):
                    tsl = slice(tt * 128, (tt + 1) * 128)
                    for c in range(DC):
                        P.dma("sync", xfm[:, c, :], io["xn232"][c, :, tsl],
                              w="2xfm")
                    for c in range(DC):
                        P.step("tensor", lambda E, c=c: E.transpose(ps_tx[:],
                               xfm[:, c, :], idf3[:]), r=("2xfm", "2idf3", "2ptx"),
                               w="2ptx")
                        P.step("vector", lambda E, c=c: E.tensor_copy(
                            xt[:, c * 128:(c + 1) * 128], ps_tx[:]),
                            r="2ptx", w=("2xt", "2ptx"))
                    fp4_quant(P, "2fq", xt[:], xqt[:], 1024, "2xt", "2xqt",
                              tq, av, lv, sc, rsc)
                    for c in range(DC):
                        P.step("tensor", lambda E, c=c: E.transpose(ps_tb[:],
                               xqt[:, c * 128:(c + 1) * 128], idb3[:]),
                               r=("2xqt", "2idb3", "2ptb"), w="2ptb")
                        P.step("vector", lambda E, c=c: E.tensor_copy(
                            xqf[:, c, :], ps_tb[:]), r="2ptb", w=("2xqf", "2ptb"))
                    for nb_ in range(6):
                        nn = 512 if nb_ < 5 else 256
                        n0 = nb_ * 512
                        for c in range(DC):
                            P.step("tensor", lambda E, c=c, n0=n0, nn=nn: E.matmul(
                                ps_g[0:128, 0:nn], xqf[:, c, :],
                                wgq[:, c, n0:n0 + nn], start=(c == 0),
                                stop=(c == DC - 1)), r=("2xqf", "2wgq", "2pg"),
                                w="2pg")
                        for c in range(DC):
                            P.step("tensor", lambda E, c=c, n0=n0, nn=nn: E.matmul(
                                ps_u[0:128, 0:nn], xqf[:, c, :],
                                wuq[:, c, n0:n0 + nn], start=(c == 0),
                                stop=(c == DC - 1)), r=("2xqf", "2wuq", "2pu3"),
                                w="2pu3")
                        P.step("scalar", lambda E, nn=nn: E.activation(
                            out=gt_[:, 0:nn], in_=ps_g[0:128, 0:nn], func=AF.Silu),
                            r="2pg", w=("2gt", "2pg"))
                        P.step("vector", lambda E, nn=nn, n0=n0: E.tensor_tensor(
                            out=htm[:, n0:n0 + nn], in0=gt_[:, 0:nn],
                            in1=ps_u[0:128, 0:nn], op=ALU.mult),
                            r=("2gt", "2pu3"), w=("2htm", "2pu3"))
                    fp4_quant(P, "2fh", htm[:], hqt[:], 2816, "2htm", "2hqt",
                              tq, av, lv, sc, rsc)
                    for ci in range(NI):
                        P.step("tensor", lambda E, ci=ci: E.transpose(ps_tb[:],
                               hqt[:, ci * 128:(ci + 1) * 128], idb3[:]),
                               r=("2hqt", "2idb3", "2ptb"), w="2ptb")
                        P.step("vector", lambda E, ci=ci: E.tensor_copy(
                            hqf[:, ci, :], ps_tb[:]), r="2ptb", w=("2hqf", "2ptb"))
                    for oc in range(DC):
                        for ci in range(NI):
                            P.step("tensor", lambda E, oc=oc, ci=ci: E.matmul(
                                ps_d[:], wdq[:, ci, oc * 128:(oc + 1) * 128],
                                hqf[:, ci, :], start=(ci == 0), stop=(ci == NI - 1)),
                                r=("2wdq", "2hqf", "2pd"), w="2pd")
                        P.step("vector", lambda E: E.tensor_copy(fot[:], ps_d[:]),
                               r="2pd", w=("2fot", "2pd"))
                        P.dma("sync", io["f2"][oc, :, tsl], fot[:], r="2fot",
                              w="2f2d")
    if "ffn" in phases:
        _l2_ffn()

    if "cmb" in phases:
        with P.scope():
            silu_combine_store(ctx, io["x2i"], io["s2"], io["f2"], io["x3o"])


L0_SHAPES = {
    "aw_qkv": [1024, 3072], "ab_qk": [128, 16], "ab_vrep": [128, 1040],
    "aw_o": [1024, 1024], "ab_o": [128, 8], "ln1w": [128, 8],
    "selq": ([128, 128], BF16), "ehrep": [16, 2048], "ehrep_f": [16, 2048],
    "fw_q": [1024, 1024], "fb_q": [128, 8], "fw_k": [1024, 512], "fb_k": [128, 4],
    "fw_v": [1024, 256], "fb_vrep": [128, 260], "qnw": [128, 1], "knw": [128, 1],
    "cos2": [64, 2048], "sin2": [64, 2048], "self_": ([128, 128], BF16),
    "selk_": ([128, 64], BF16), "eh2f": [16, 1024], "eh2k": [16, 512],
    "rep4": [4, 16], "pq": ([128, 128], BF16), "pk": ([128, 128], BF16),
    "eh2p": [16, 1024], "qnwp": [128, 1], "knwp": [128, 1],
    "selkp": ([128, 64], BF16), "maskc": ([128, 896], BF16),
    "id128b": ([128, 128], BF16),
    "wg": [64, 64], "bg": [64, 1], "fw_o": [1024, 1024], "fb_o": [128, 8],
    "ln2w": [128, 8], "wr": [1024, 8], "brrep": [128, 8],
    "w1": [8, 1024, 4096], "b1c": [8, 128, 32], "w2": [8, 2048, 1024],
    "b2sb": [8, 1024], "erow": ([8, 1024], BF16),
}
L0_INTER = (("xn", None, BF16), ("xn32", None, F32), ("h", None, BF16),
            ("s0", None, F32), ("xmid", None, F32), ("h2", None, BF16),
            ("y", None, F32))
L1_SHAPES = {
    "l1_wq": [1024, 1024], "l1_wk": [1024, 1024], "l1_wv": [1024, 1024],
    "l1_wc": [1024, 1024], "l1_gate": [1024, 8], "l1_cbrep": [128, 8],
    "l1_w1": [8, 512, 512], "l1_w2": [8, 512, 512], "l1_fc1": [1024, 512],
    "l1_fc2": [512, 1024], "l1_up": [1024, 4096], "l1_down": [4096, 1024],
    "iota8": [128, 8], "ones8": [128, 8], "id128b": ([128, 128], BF16),
    "erowf": [8, 1024],
}
L1_INTER = (("xn1", None, BF16), ("s1", None, F32), ("f1", None, F32))
L2_SHAPES = {
    "l2_wil": [1024, 2048], "l2_rot": [64, 16 * 128], "iota32": [128, 128],
    "iota64c": [128, 1], "ps2": [128, 2], "hlfc": [128, 1], "lt2": [128, 128],
    "selpr": [4, 256], "idf32": [128, 128], "id128b": ([128, 128], BF16),
    "l2_wo": [1024, 1024], "l2_wg": [2816, 1024], "l2_wu": [2816, 1024],
    "l2_wd": [1024, 2816],
}
L2_INTER = (("xn2", None, BF16), ("xn232", None, F32),
            ("qkvil", [16, 128, 4096], BF16), ("qkvs", [2 * 16 * 8192, 128], BF16),
            ("posd", [32, 128, 64], F32), ("otm", [8192, 65], F32),
            ("attnil", [16, 64, 4096], BF16), ("s2", None, F32),
            ("f2", None, F32))
L3_SHAPES = {
    "l3_wqkv": [1024, 3072], "l3_wo": [1024, 1024], "l3_lnw": [128, 8],
    "l3_lnb": [128, 8], "maskoct": ([128, 128], BF16),
    "l3_w1t": [1024, 4096], "l3_b1c": [128, 32], "l3_w2t": [4096, 1024],
    "l3_b2c": [128, 8], "selq": ([128, 128], BF16), "ehrep_f": [16, 2048],
}
L3_INTER = (("xn3", None, BF16), ("xn332", None, F32), ("xin3", None, BF16),
            ("s3", None, F32), ("f3", None, F32),
            ("w1qd", [1024, 4096], BF16), ("w2qd", [4096, 1024], BF16))


def _mk_prog(builder, in_name, out_name, shapes, inter):
    P = Prog()
    io = {}
    io[in_name] = P.dram(in_name, [8, 128, T], F32)
    for k, sp in shapes.items():
        if isinstance(sp, tuple):
            io[k] = P.dram(k, sp[0], sp[1])
        else:
            io[k] = P.dram(k, sp, F32)
    for nm, shp, dt in inter:
        io[nm] = P.dram(nm, shp if shp is not None else [8, 128, T], dt,
                        kind="Internal")
    io[out_name] = P.dram(out_name, [8, 128, T], F32, kind="ExternalOutput")
    ctx = Ctx(P)
    builder(P, ctx, io)
    P.barrier()
    return P.emit()


def _reshard_pos(xbsd, core):
    xp = np.asarray(xbsd)[:, 256 * core:256 * (core + 1), :]
    xp = np.ascontiguousarray(xp.transpose(1, 0, 2).reshape(4096, 1024))
    return np.ascontiguousarray(xp.T.reshape(8, 128, 4096))


def _unreshard_pos(xfm):
    xt = np.asarray(xfm).reshape(1024, 4096).T
    return xt.reshape(256, 16, 1024).transpose(1, 0, 2)


def _launch(builder, in_name, out_name, shapes, inter, consts, xs_fm):
    nc = _mk_prog(builder, in_name, out_name, shapes, inter)
    inmaps = []
    for c in range(8):
        im = {in_name: xs_fm[c]}
        for k in shapes:
            im[k] = np.ascontiguousarray(consts[k])
        inmaps.append(im)
    res = run_bass_kernel_spmd(nc, inmaps, core_ids=list(range(8)))
    return [np.asarray(res.results[c][out_name]) for c in range(8)]


def kernel(**z):
    z = {k: np.asarray(v) for k, v in z.items()}
    x = z["L0_attn_x"].astype(np.float32)
    consts = {}
    consts.update(prep_layer2_consts(z))
    consts.update(prep_layer3_consts(z))
    consts.update(prep_layer1_consts(z))
    consts.update(prep_layer0_consts(z))

    xs = [_to_fm(x[2 * c:2 * c + 2]) for c in range(8)]
    xs = _launch(build_layer0, "x", "x_out", L0_SHAPES, L0_INTER, consts, xs)
    xs = _launch(build_layer1, "x1", "x2", L1_SHAPES, L1_INTER, consts, xs)
    xs = _launch(build_layer2, "x2i", "x3o", L2_SHAPES, L2_INTER, consts, xs)
    x3 = np.concatenate([_from_fm(xs[c], 2, 2048) for c in range(8)], axis=0)
    xs = [_reshard_pos(x3, c) for c in range(8)]
    xs = _launch(build_layer3, "x2p", "x3", L3_SHAPES, L3_INTER, consts, xs)
    out = np.empty((16, 2048, 1024), np.float32)
    for c in range(8):
        out[:, 256 * c:256 * (c + 1), :] = _unreshard_pos(xs[c])
    return out
